# version 12
# baseline (speedup 1.0000x reference)
from contextlib import ExitStack

import numpy as np
import ml_dtypes

import concourse.bass as bass
import concourse.mybir as mybir
from concourse.bass_utils import run_bass_kernel_spmd

F32 = mybir.dt.float32
BF16 = mybir.dt.bfloat16
AF = mybir.ActivationFunctionType
ALU = mybir.AluOpType
AXX = mybir.AxisListType.X

NCORES = 8
S = 16384
SC = 2048
HALO = 128
SH = SC + HALO
D = 1024
E = 2048
DFF = 4096
NT_H = SH // 128
NT_R = SC // 128
EPS = 1e-5

C32 = 12582912.0
C81 = 1.5 * 2.0 ** 81
P29 = 2.0 ** 29
IP29 = 2.0 ** -29
INV127 = 1.0 / 127.0
INV32767 = 1.0 / 32767.0


def _nchunks(total, step=512):
    out, o = [], 0
    while o < total:
        w = min(step, total - o)
        out.append((o, w))
        o += w
    return out


class Trk:

    ENGS = ("sync", "tensor", "vector", "scalar", "gpsimd")

    def __init__(self, nc, ctx):
        self.nc = nc
        self.ctx = ctx
        self.sem = {e: ctx.enter_context(nc.semaphore("S_" + e)) for e in self.ENGS}
        self.cnt = {e: 0 for e in self.ENGS}
        self.waited = {e: {} for e in self.ENGS}
        self.bufs = {}
        self._dma_sems = {}

    def _dma_sem(self, key):
        sk = "D_" + str(key)
        if sk not in self.sem:
            self.sem[sk] = self.ctx.enter_context(self.nc.semaphore(sk))
            self.cnt[sk] = 0
        return sk

    def op(self, eng, fn, reads=(), writes=(), dma=False):
        sk = self._dma_sem(writes[0] if writes else "st") if dma else eng
        deps = []
        for k in reads:
            b = self.bufs.get(k)
            if b and b[0] is not None:
                deps.append(b[0])
        for k in writes:
            b = self.bufs.get(k)
            if b:
                if b[0] is not None:
                    deps.append(b[0])
                deps.extend(b[1].items())
        need = {}
        for se, c in deps:
            if se == eng == "tensor":
                continue
            if se.startswith("D_"):
                c = self.cnt[se]
            if c > self.waited[eng].get(se, 0):
                need[se] = max(need.get(se, 0), c)
        e = getattr(self.nc, eng)
        for se in sorted(need):
            e.wait_ge(self.sem[se], need[se])
            self.waited[eng][se] = need[se]
        ins = fn(e)
        inc = 16 if dma else 1
        ins.then_inc(self.sem[sk], inc)
        self.cnt[sk] += inc
        c = self.cnt[sk]
        for k in writes:
            self.bufs[k] = [(sk, c), {}]
        for k in reads:
            b = self.bufs.setdefault(k, [None, {}])
            b[1][sk] = c
        return ins

    def load(self, out_ap, in_ap, writes, reads=(), eng="sync"):
        return self.op(eng, lambda e: e.dma_start(out=out_ap, in_=in_ap),
                       reads=reads, writes=writes, dma=True)

    def store(self, out_ap, in_ap, reads, writes=(), eng="gpsimd"):
        return self.op(eng, lambda e: e.dma_start(out=out_ap, in_=in_ap),
                       reads=reads, writes=writes, dma=True)

    def finish(self):
        for eng in ("sync", "gpsimd"):
            e = getattr(self.nc, eng)
            for se in self.cnt:
                if self.cnt[se] > self.waited[eng].get(se, 0):
                    e.wait_ge(self.sem[se], self.cnt[se])
                    self.waited[eng][se] = self.cnt[se]


class Prog:
    def __init__(self, name):
        self.nc = bass.Bass("TRN2", target_bir_lowering=False)
        self.ctx = ExitStack()
        self.t = Trk(self.nc, self.ctx)
        self.name = name

    def inp(self, name, shape, dt=F32):
        return self.nc.dram_tensor(name, list(shape), dt, kind="ExternalInput").ap()

    def out(self, name, shape, dt=F32):
        return self.nc.dram_tensor(name, list(shape), dt, kind="ExternalOutput").ap()

    def scratch(self, name, shape, dt=F32):
        return self.nc.dram_tensor(name, list(shape), dt, kind="Internal").ap()

    def sb(self, name, shape, dt=F32, stack=None, side=None):
        return (stack or self.ctx).enter_context(
            self.nc.sbuf_tensor(name, list(shape), dt, side=side))

    def ps(self, name, shape, dt=F32):
        return self.ctx.enter_context(self.nc.psum_tensor(name, list(shape), dt))

    def barrier(self):
        t = self.t
        for eng in Trk.ENGS:
            e = getattr(self.nc, eng)
            for se in t.cnt:
                if t.cnt[se] > t.waited[eng].get(se, 0):
                    e.wait_ge(t.sem[se], t.cnt[se])
                    t.waited[eng][se] = t.cnt[se]

    def scope(self):
        return ExitStack()

    def close(self):
        self.t.finish()
        self.ctx.close()
        return self.nc


def bcast_col(col_ap, n):
    p0 = list(col_ap.ap[0])
    return bass.AP(col_ap.tensor, col_ap.offset, [p0, [0, n]])



def emit_identity(p, dt=BF16, key="ident"):
    t = p.t
    ident = p.sb(key, [128, 128], dt)
    t.op("gpsimd", lambda e: e.memset(ident[:, :], 0.0), writes=(key,))
    t.op("gpsimd", lambda e: e.affine_select(
        out=ident[:, :], in_=ident[:, :], compare_op=ALU.not_equal,
        fill=1.0, base=0, pattern=[[-1, 128]], channel_multiplier=1),
        reads=(key,), writes=(key,))
    return ident


def emit_wbase(p, ones_row, w_col, psc):
    t = p.t
    dst = p.sb("wbase", [128, 2], F32)
    t.op("vector", lambda e: e.tensor_scalar(
        out=p._wrow[:, 1:2], in0=w_col, scalar1=-1.0, scalar2=1.0,
        op0=ALU.mult, op1=ALU.add), reads=("wit",), writes=("wit",))
    t.op("vector", lambda e: e.tensor_scalar(
        out=p._wrow[:, 0:1], in0=w_col, scalar1=IP29, scalar2=None,
        op0=ALU.mult), reads=("wit",), writes=("wit",))
    t.op("tensor", lambda e: e.matmul(
        psc[:, 0:2], ones_row[:, :], p._wrow[0:1, 0:2], start=True, stop=True),
        reads=("ones", "wit"), writes=("pscal",))
    t.op("scalar", lambda e: e.activation(out=dst[:, :], in_=psc[:, 0:2],
                                          func=AF.Copy),
         reads=("pscal",), writes=("wit_b",))
    return dst[:, 1:2], dst[:, 0:1]


def emit_witness(p):
    t = p.t
    ones = p.sb("ones_row", [1, 128], F32)
    t.op("vector", lambda e: e.memset(ones[:, :], 1.0), writes=("ones",))
    wt = p.sb("wit", [1, 2], F32)
    t.op("vector", lambda e: e.memset(wt[:, 0:1], 0.75), writes=("wit",))
    t.op("vector", lambda e: e.tensor_scalar(
        out=wt[:, 1:2], in0=wt[:, 0:1], scalar1=P29, scalar2=C81,
        op0=ALU.mult, op1=ALU.add), reads=("wit",), writes=("wit",))
    t.op("vector", lambda e: e.tensor_scalar(
        out=wt[:, 0:1], in0=wt[:, 1:2], scalar1=C81, scalar2=IP29,
        op0=ALU.subtract, op1=ALU.mult), reads=("wit",), writes=("wit",))
    p._wrow = p.sb("wrow", [1, 2], F32)
    return ones, wt[:, 0:1]


def emit_ln_tile(p, x_ap, x_key, tmp, stat, sfx=""):
    t = p.t
    ks, kt = "lnstat" + sfx, "lntmp" + sfx
    m, r = stat[:, 0:1], stat[:, 1:2]
    t.op("vector", lambda e: e.tensor_reduce(
        out=stat[:, 4:5], in_=x_ap, axis=AXX, op=ALU.add),
        reads=(x_key,), writes=(ks,))
    t.op("vector", lambda e: e.tensor_scalar(
        out=m, in0=stat[:, 4:5], scalar1=1.0 / D, scalar2=None, op0=ALU.mult),
        reads=(ks,), writes=(ks,))
    t.op("vector", lambda e: e.scalar_tensor_tensor(
        out=tmp, in0=x_ap, scalar=0.0, in1=x_ap, op0=ALU.add, op1=ALU.mult,
        accum_out=stat[:, 2:3]), reads=(x_key,), writes=(kt, ks))
    t.op("vector", lambda e: e.tensor_tensor(
        out=stat[:, 5:6], in0=m, in1=m, op=ALU.mult),
        reads=(ks,), writes=(ks,))
    t.op("vector", lambda e: e.tensor_scalar(
        out=stat[:, 3:4], in0=stat[:, 2:3], scalar1=1.0 / D, scalar2=EPS,
        op0=ALU.mult, op1=ALU.add), reads=(ks,), writes=(ks,))
    t.op("vector", lambda e: e.tensor_tensor(
        out=stat[:, 3:4], in0=stat[:, 3:4], in1=stat[:, 5:6], op=ALU.subtract),
        reads=(ks,), writes=(ks,))
    t.op("scalar", lambda e: e.activation(
        out=stat[:, 6:7], in_=stat[:, 3:4], func=AF.Ln),
        reads=(ks,), writes=(ks,))
    t.op("scalar", lambda e: e.activation(
        out=r, in_=stat[:, 6:7], func=AF.Exp, scale=-0.5),
        reads=(ks,), writes=(ks,))
    return m, r


def emit_quant(p, src_ap, src_key, dst_ap, dst_key, scale_col, omwrc_col,
               w29rc_col, tmp1, tmp2, extra_reads=(), skip_scale=False):
    t = p.t
    er = tuple(extra_reads)
    if not skip_scale:
        t.op("vector", lambda e: e.tensor_scalar(
            out=tmp1, in0=src_ap, scalar1=scale_col, scalar2=None, op0=ALU.mult),
            reads=(src_key, "wit_b") + er, writes=("qt1",))
    t.op("vector", lambda e: e.tensor_scalar(
        out=tmp2, in0=tmp1, scalar1=C32, scalar2=-C32, op0=ALU.add, op1=ALU.add),
        reads=("qt1",), writes=("qt2",))
    t.op("vector", lambda e: e.tensor_scalar(
        out=tmp1, in0=tmp1, scalar1=P29, scalar2=C81, op0=ALU.mult, op1=ALU.add),
        reads=("qt1",), writes=("qt1",))
    t.op("vector", lambda e: e.tensor_scalar(
        out=tmp1, in0=tmp1, scalar1=C81, scalar2=w29rc_col, op0=ALU.subtract,
        op1=ALU.mult), reads=("qt1", "wit_b") + er, writes=("qt1",))
    t.op("vector", lambda e: e.scalar_tensor_tensor(
        out=dst_ap, in0=tmp2, scalar=omwrc_col, in1=tmp1, op0=ALU.mult,
        op1=ALU.add), reads=("qt1", "qt2", "wit_b") + er, writes=(dst_key,))


def emit_transpose(p, src_ap, src_key, pbank, pkey, dst_ap, dst_key,
                   ident, ident_key="ident", act_kwargs=None):
    t = p.t
    t.op("tensor", lambda e: e.transpose(pbank, src_ap, ident[:, :]),
         reads=(src_key, ident_key), writes=(pkey,))
    kw = act_kwargs or {}
    t.op("scalar", lambda e: e.activation(out=dst_ap, in_=pbank, func=AF.Copy, **kw),
         reads=(), writes=(pkey, dst_key))


def emit_amax_scalar(p, mx_col, mx_key, o_ap, scrname):
    t = p.t
    scr = p.scratch(scrname, [128, 1], F32)
    row = p.sb(scrname + "_row", [1, 130], F32)
    t.store(scr, mx_col, reads=(mx_key,), writes=(scrname + "_d",), eng="gpsimd")
    t.load(row[0:1, 0:128], scr.rearrange("a b -> b a"),
           writes=(scrname,), reads=(scrname + "_d",), eng="gpsimd")
    t.op("vector", lambda e: e.tensor_reduce(
        out=row[0:1, 128:129], in_=row[0:1, 0:128], axis=AXX, op=ALU.max),
        reads=(scrname,), writes=(scrname,))
    t.store(o_ap, row[0:1, 128:129], reads=(scrname,), eng="gpsimd")


def emit_scales(p, am_ap, am_name, ones_row, w_col, psc):
    t = p.t
    row = p.sb(am_name + "_r", [1, 16], F32)
    t.load(row[0:1, 0:8], am_ap, writes=(am_name,))
    t.op("vector", lambda e: e.tensor_reduce(
        out=row[0:1, 8:9], in_=row[0:1, 0:8], axis=AXX, op=ALU.max),
        reads=(am_name,), writes=(am_name,))
    t.op("vector", lambda e: e.tensor_scalar(
        out=row[0:1, 9:10], in0=row[0:1, 8:9], scalar1=INV32767, scalar2=None,
        op0=ALU.mult), reads=(am_name,), writes=(am_name,))
    t.op("vector", lambda e: e.reciprocal(
        out=row[0:1, 10:11], in_=row[0:1, 9:10]),
        reads=(am_name,), writes=(am_name,))
    t.op("vector", lambda e: e.tensor_copy(out=row[0:1, 11:12], in_=w_col),
         reads=("wit",), writes=(am_name,))
    dst = p.sb(am_name + "_b", [128, 5], F32)
    t.op("tensor", lambda e: e.matmul(
        psc[:, 0:3], ones_row[:, :], row[0:1, 9:12], start=True, stop=True),
        reads=("ones", am_name), writes=("pscal",))
    t.op("scalar", lambda e: e.activation(out=dst[:, 0:3], in_=psc[:, 0:3],
                                          func=AF.Copy),
         reads=("pscal",), writes=("wit_b",))
    t.op("vector", lambda e: e.tensor_scalar(
        out=dst[:, 3:4], in0=dst[:, 2:3], scalar1=-1.0, scalar2=1.0,
        op0=ALU.mult, op1=ALU.add), reads=("wit_b",), writes=("wit_b",))
    t.op("vector", lambda e: e.tensor_tensor(
        out=dst[:, 3:4], in0=dst[:, 3:4], in1=dst[:, 0:1], op=ALU.mult),
        reads=("wit_b",), writes=("wit_b",))
    t.op("vector", lambda e: e.tensor_scalar(
        out=dst[:, 4:5], in0=dst[:, 2:3], scalar1=IP29, scalar2=None,
        op0=ALU.mult), reads=("wit_b",), writes=("wit_b",))
    t.op("vector", lambda e: e.tensor_tensor(
        out=dst[:, 4:5], in0=dst[:, 4:5], in1=dst[:, 0:1], op=ALU.mult),
        reads=("wit_b",), writes=("wit_b",))
    return dst[:, 1:2], dst[:, 3:4], dst[:, 4:5]


def emit_yq(p, xs, sc, omwrc, w29rc, ident, yq, ntiles, pT, stack=None):
    t = p.t
    xt = [p.sb(f"qx{i}", [128, D], stack=stack) for i in range(2)]
    tmp = p.sb("qtmp", [128, D], stack=stack)
    t1 = p.sb("qtmp1", [128, D], stack=stack)
    t2 = p.sb("qtmp2", [128, D], stack=stack)
    yqt = p.sb("yqt", [128, D], BF16, stack=stack)
    stat = p.sb("qstat", [128, 8], stack=stack)
    for i in range(ntiles):
        b = i % 2
        xk = f"qx{b}"
        t.load(xt[b][:, :], xs[i * 128:(i + 1) * 128, :], writes=(xk,))
        m, r = emit_ln_tile(p, xt[b][:, :], xk, tmp[:, :], stat)
        t.op("vector", lambda e: e.tensor_tensor(
            out=stat[:, 7:8], in0=r, in1=sc, op=ALU.mult),
            reads=("lnstat", "wit_b"), writes=("lnstat",))
        t.op("vector", lambda e: e.tensor_scalar(
            out=t1[:, :], in0=xt[b][:, :], scalar1=m, scalar2=stat[:, 7:8],
            op0=ALU.subtract, op1=ALU.mult),
            reads=(xk, "lnstat"), writes=("qt1",))
        emit_quant(p, None, None, yqt[:, :], "yqt", sc, omwrc, w29rc,
                   t1[:, :], t2[:, :], skip_scale=True)
        for k in range(8):
            pb = k % 2
            emit_transpose(p, yqt[:, k * 128:(k + 1) * 128], "yqt",
                           pT[pb][:, :], f"pT{pb}",
                           yq[k][:, i * 128:(i + 1) * 128], f"yq{k}", ident)


def emit_w1t_quant(p, w1t, omwn, w29n, ident, sink, pT, stack=None):
    t = p.t
    wtmp = p.sb("wtmp", [128, D], stack=stack)
    wa = p.sb("wa", [128, 6], stack=stack)
    wt1 = p.sb("wq1", [128, D], stack=stack)
    wt2 = p.sb("wq2", [128, D], stack=stack)
    wq = p.sb("wq", [128, D], BF16, stack=stack)
    pW = pT
    for t32 in range(DFF // 128):
        t.load(wtmp[:, :], w1t[t32 * 128:(t32 + 1) * 128, :], writes=("wtmp",))
        t.op("vector", lambda e: e.tensor_reduce(
            out=wa[:, 0:1], in_=wtmp[:, :], axis=AXX, op=ALU.max,
            apply_absolute_value=True), reads=("wtmp",), writes=("wa",))
        t.op("vector", lambda e: e.tensor_scalar(
            out=wa[:, 1:2], in0=wa[:, 0:1], scalar1=INV127, scalar2=None,
            op0=ALU.mult), reads=("wa",), writes=("wa",))
        t.op("vector", lambda e: e.reciprocal(out=wa[:, 2:3], in_=wa[:, 1:2]),
             reads=("wa",), writes=("wa",))
        t.op("vector", lambda e: e.tensor_tensor(
            out=wa[:, 3:4], in0=omwn, in1=wa[:, 1:2], op=ALU.mult),
            reads=("wa", "wit_b"), writes=("wa",))
        t.op("vector", lambda e: e.tensor_tensor(
            out=wa[:, 4:5], in0=w29n, in1=wa[:, 1:2], op=ALU.mult),
            reads=("wa", "wit_b"), writes=("wa",))
        emit_quant(p, wtmp[:, :], "wtmp", wq[:, :], "wq", wa[:, 2:3],
                   wa[:, 3:4], wa[:, 4:5], wt1[:, :], wt2[:, :],
                   extra_reads=("wa",))
        for k in range(8):
            pb = k % 2
            dst_ap, dst_key = sink(k, t32)
            emit_transpose(p, wq[:, k * 128:(k + 1) * 128], "wq",
                           pW[pb][:, :], f"pT{pb}", dst_ap, dst_key, ident)


def emit_h_pass(p, yq, l1t, binf, ntiles, halo_tiles, hq_sink, ht_dt=BF16):
    t = p.t
    ht = [p.sb(f"ht{i}", [128, 512], ht_dt) for i in range(2)]
    pbk = [p.ps(f"ph{i}", [128, 512], F32) for i in range(4)]
    mx = p.sb("hmx", [128, 2])
    t.op("vector", lambda e: e.memset(mx[:, 0:1], 0.0), writes=("hmx",))
    chunks = _nchunks(ntiles * 128)
    for mi in range(DFF // 128):
        for ci, (n0, w) in enumerate(chunks):
            it = mi * len(chunks) + ci
            b = it % 4
            b2 = it % 2
            pk = f"ph{b}"
            for k in range(8):
                t.op("tensor", lambda e, k=k, b=b: e.matmul(
                    pbk[b][:, 0:w], l1t[k][:, mi * 128:(mi + 1) * 128],
                    yq[k][:, n0:n0 + w], start=(k == 0), stop=(k == 7)),
                    reads=(f"l1t{k}", f"yq{k}"), writes=(pk,))
            t.op("scalar", lambda e, b=b, b2=b2, w=w: e.activation(
                out=ht[b2][:, 0:w], in_=pbk[b][:, 0:w], func=AF.Gelu,
                bias=binf[:, mi:mi + 1], scale=1.0),
                reads=(pk, "b_in_f"), writes=(pk, f"ht{b2}"))
            lo = max(0, halo_tiles * 128 - n0)
            if lo < w:
                t.op("vector", lambda e, b2=b2, lo=lo, w=w: e.tensor_reduce(
                    out=mx[:, 1:2], in_=ht[b2][:, lo:w], axis=AXX, op=ALU.max,
                    apply_absolute_value=True), reads=(f"ht{b2}",), writes=("hmx",))
                t.op("vector", lambda e: e.tensor_tensor(
                    out=mx[:, 0:1], in0=mx[:, 0:1], in1=mx[:, 1:2], op=ALU.max),
                    reads=("hmx",), writes=("hmx",))
            if hq_sink is not None:
                hq_sink(mi, n0, w, ht[b2][:, 0:w], f"ht{b2}")
    return mx[:, 0:1], "hmx"



def build_l1():
    p = Prog("l1")
    xs = p.inp("xs", [SH, D])
    o_a1 = p.out("o_a1", [1, 1])
    xt = [p.sb(f"xt{i}", [128, D]) for i in range(2)]
    tmp = [p.sb(f"tmp{i}", [128, D]) for i in range(2)]
    y = [p.sb(f"y{i}", [128, D]) for i in range(2)]
    stat = [p.sb(f"stat{i}", [128, 8]) for i in range(2)]
    mx = p.sb("mx", [128, 5])
    t = p.t
    t.op("vector", lambda e: e.memset(mx[:, 0:2], 0.0), writes=("mxp0", "mxp1"))
    for i in range(NT_H):
        b = i % 2
        xk = f"xt{b}"
        t.load(xt[b][:, :], xs[i * 128:(i + 1) * 128, :], writes=(xk,))
        m, r = emit_ln_tile(p, xt[b][:, :], xk, tmp[b][:, :], stat[b], sfx=str(b))
        t.op("vector", lambda e: e.tensor_scalar(
            out=y[b][:, :], in0=xt[b][:, :], scalar1=m, scalar2=r,
            op0=ALU.subtract, op1=ALU.mult),
            reads=(xk, "lnstat" + str(b)), writes=(f"y{b}",))
        t.op("vector", lambda e, b=b: e.tensor_reduce(
            out=mx[:, 3 + b:4 + b], in_=y[b][:, :], axis=AXX, op=ALU.max,
            apply_absolute_value=True), reads=(f"y{b}",), writes=(f"mxt{b}",))
        t.op("vector", lambda e, b=b: e.tensor_tensor(
            out=mx[:, b:b + 1], in0=mx[:, b:b + 1], in1=mx[:, 3 + b:4 + b],
            op=ALU.max), reads=(f"mxt{b}", f"mxp{b}"), writes=(f"mxp{b}",))
    t.op("vector", lambda e: e.tensor_tensor(
        out=mx[:, 2:3], in0=mx[:, 0:1], in1=mx[:, 1:2], op=ALU.max),
        reads=("mxp0", "mxp1"), writes=("mx",))
    emit_amax_scalar(p, mx[:, 2:3], "mx", o_a1, "scr1")
    return p.close()


def build_l2(ntiles, halo_tiles, name):
    p = Prog(name)
    ST = ntiles * 128
    xs = p.inp("xs", [ST, D])
    am1 = p.inp("am1", [1, 8])
    w1t = p.inp("w1t", [DFF, D])
    b_in = p.inp("b_in_f", [128, DFF // 128])
    o_a2 = p.out("o_a2", [1, 1])
    o_h = p.out("o_h", [DFF, ST])
    t = p.t
    ident = emit_identity(p)
    ones, w_col = emit_witness(p)
    binf = p.sb("binf", [128, DFF // 128])
    t.load(binf[:, :], b_in, writes=("b_in_f",))
    psc = p.ps("pscal", [128, 8], F32)
    pT = [p.ps(f"pT{i}", [128, 128], BF16) for i in range(2)]
    omwn, w29n = emit_wbase(p, ones, w_col, psc)
    sc, omwrc, w29rc = emit_scales(p, am1, "am1", ones, w_col, psc)
    yq = [p.sb(f"yq{k}", [128, ntiles * 128], BF16) for k in range(8)]
    emit_yq(p, xs, sc, omwrc, w29rc, ident, yq, ntiles, pT)
    l1t = [p.sb(f"l1t{k}", [128, DFF], BF16) for k in range(8)]

    def sink(k, t32):
        return l1t[k][:, t32 * 128:(t32 + 1) * 128], f"l1t{k}"

    emit_w1t_quant(p, w1t, omwn, w29n, ident, sink, pT)

    def h_sink(mi, n0, wd, h_ap, h_key):
        p.t.store(
            bass.AP(o_h.tensor, mi * 128 * ST + n0, [[ST, 128], [1, wd]]),
            h_ap, reads=(h_key,))

    mxc, mxk = emit_h_pass(p, yq, l1t, binf, ntiles, halo_tiles, h_sink,
                           ht_dt=F32)
    emit_amax_scalar(p, mxc, mxk, o_a2, "scr2")
    return p.close()



def emit_w1o_quant_to_scratch(p, w1o, omwn, w29n, w1oq_sb, stack=None):
    t = p.t
    wtmp = p.sb("owtmp", [128, D], stack=stack)
    wa = p.sb("owa", [128, 6], stack=stack)
    wt1 = p.sb("owq1", [128, D], stack=stack)
    wt2 = p.sb("owq2", [128, D], stack=stack)
    for k32 in range(DFF // 128):
        t.load(wtmp[:, :], w1o[k32 * 128:(k32 + 1) * 128, :], writes=("owtmp",))
        t.op("vector", lambda e: e.tensor_reduce(
            out=wa[:, 0:1], in_=wtmp[:, :], axis=AXX, op=ALU.max,
            apply_absolute_value=True), reads=("owtmp",), writes=("owa",))
        t.op("vector", lambda e: e.tensor_scalar(
            out=wa[:, 1:2], in0=wa[:, 0:1], scalar1=INV127, scalar2=None,
            op0=ALU.mult), reads=("owa",), writes=("owa",))
        t.op("vector", lambda e: e.reciprocal(out=wa[:, 2:3], in_=wa[:, 1:2]),
             reads=("owa",), writes=("owa",))
        t.op("vector", lambda e: e.tensor_tensor(
            out=wa[:, 3:4], in0=omwn, in1=wa[:, 1:2], op=ALU.mult),
            reads=("owa", "wit_b"), writes=("owa",))
        t.op("vector", lambda e: e.tensor_tensor(
            out=wa[:, 4:5], in0=w29n, in1=wa[:, 1:2], op=ALU.mult),
            reads=("owa", "wit_b"), writes=("owa",))
        emit_quant(p, wtmp[:, :], "owtmp", w1oq_sb[k32][:, :], f"w1oq{k32}",
                   wa[:, 2:3], wa[:, 3:4], wa[:, 4:5], wt1[:, :], wt2[:, :],
                   extra_reads=("owa",))


def build_ffn_out(name, ntiles, mamba_prep):
    p = Prog(name)
    ST = ntiles * 128
    xs = p.inp("xs", [ST, D])
    i_h = p.inp("i_h", [DFF, ST])
    am2 = p.inp("am2", [1, 8])
    w1o = p.inp("w1o", [DFF, D])
    b_out_row = p.inp("b_out_row", [1, D])
    if mamba_prep:
        o_xp = p.out("o_xp", [ST, D])
        o_z1 = p.out("o_z1", [SC, D])
        o_z2 = p.out("o_z2", [D, ST])
        ln_g = p.inp("ln_g", [128, 8])
        ln_b = p.inp("ln_b", [128, 8])
    else:
        o_out = p.out("o_out", [ST, D])
    t = p.t
    identf = emit_identity(p, dt=F32, key="identf")
    ones, w_col = emit_witness(p)
    borow = p.sb("borow", [1, D])
    t.load(borow[:, :], b_out_row, writes=("borow",))
    borowb = p.sb("borowb", [1, D], BF16)
    t.op("vector", lambda e: e.tensor_copy(out=borowb[:, :], in_=borow[:, :]),
         reads=("borow",), writes=("borowb",))
    onesb = p.sb("onesb", [1, 128], BF16)
    t.op("vector", lambda e: e.tensor_copy(out=onesb[:, :], in_=ones[:, :]),
         reads=("ones",), writes=("onesb",))
    if mamba_prep:
        lng = p.sb("lng", [128, 8])
        lnb = p.sb("lnb", [128, 8])
        t.load(lng[:, :], ln_g, writes=("lng",))
        t.load(lnb[:, :], ln_b, writes=("lnb",))
    psc = p.ps("pscal", [128, 8], F32)
    omwn, w29n = emit_wbase(p, ones, w_col, psc)
    sc2, omwrc2, w29rc2 = emit_scales(p, am2, "am2", ones, w_col, psc)

    w1oq = [p.sb(f"w1oq{k}", [128, D], BF16) for k in range(DFF // 128)]
    sc_o = p.scope()
    emit_w1o_quant_to_scratch(p, w1o, omwn, w29n, w1oq, stack=sc_o)
    p.barrier()
    sc_o.close()

    HQ = [p.sb(f"hq{mi}", [128, 512], BF16) for mi in range(DFF // 128)]
    hldt = [p.sb(f"hld{i}", [128, 512], F32) for i in range(2)]
    ph = [p.ps(f"ph{i}", [128, 512], F32) for i in range(4)]
    pf = [p.ps(f"pf{i}", [128, 128], F32) for i in range(2)]
    qt1 = p.sb("cqt1", [128, 512])
    qt2 = p.sb("cqt2", [128, 512])
    ow = p.sb("ow", [128, D])
    xt2 = p.sb("xt2", [128, D])
    z1t = p.sb("z1t", [128, D])
    z1n = p.sb("z1n", [128, D])
    lntmp = p.sb("clntmp", [128, D])
    lstat = p.sb("clstat", [128, 8])
    chunks = _nchunks(ST)
    for ci, (n0, wd) in enumerate(chunks):
        for mi in range(DFF // 128):
            b = mi % 2
            lk = f"hld{b}"
            t.load(hldt[b][:, 0:wd],
                   bass.AP(i_h.tensor, mi * 128 * ST + n0, [[ST, 128], [1, wd]]),
                   writes=(lk,))
            emit_quant(p, hldt[b][:, 0:wd], lk, HQ[mi][:, 0:wd],
                       f"hq{mi}", sc2, omwrc2, w29rc2, qt1[:, 0:wd],
                       qt2[:, 0:wd])
        for j in range(wd // 128):
            for nn in range(2):
                bb = (j * 2 + nn) % 4
                pk = f"ph{bb}"
                for k32 in range(DFF // 128):
                    t.op("tensor", lambda e, k32=k32, nn=nn, j=j, bb=bb: e.matmul(
                        ph[bb][:, 0:512],
                        HQ[k32][:, j * 128:(j + 1) * 128],
                        w1oq[k32][:, nn * 512:(nn + 1) * 512],
                        start=(k32 == 0), stop=False),
                        reads=(f"hq{k32}", f"w1oq{k32}"), writes=(pk,))
                t.op("tensor", lambda e, nn=nn, bb=bb: e.matmul(
                    ph[bb][:, 0:512], onesb[0:1, :],
                    borowb[0:1, nn * 512:(nn + 1) * 512],
                    start=False, stop=True),
                    reads=("onesb", "borowb"), writes=(pk,))
                t.op("scalar", lambda e, nn=nn, bb=bb: e.activation(
                    out=ow[:, nn * 512:(nn + 1) * 512], in_=ph[bb][:, 0:512],
                    func=AF.Copy, scale=0.5),
                    reads=(pk,), writes=(pk, "ow"))
            gt = n0 + j * 128
            t.load(xt2[:, :], xs[gt:gt + 128, :], writes=("xt2",))
            t.op("vector", lambda e: e.tensor_tensor(
                out=ow[:, :], in0=ow[:, :], in1=xt2[:, :], op=ALU.add),
                reads=("ow", "xt2"), writes=("ow",))
            if not mamba_prep:
                t.store(o_out[gt:gt + 128, :], ow[:, :], reads=("ow",))
                continue
            t.store(o_xp[gt:gt + 128, :], ow[:, :], reads=("ow",))
            m1, r1 = emit_ln_tile(p, ow[:, :], "ow", lntmp[:, :], lstat)
            t.op("vector", lambda e: e.tensor_scalar(
                out=z1t[:, :], in0=ow[:, :], scalar1=m1, scalar2=r1,
                op0=ALU.subtract, op1=ALU.mult),
                reads=("ow", "lnstat"), writes=("z1t",))
            if gt >= HALO:
                t.store(o_z1[gt - HALO:gt - HALO + 128, :], z1t[:, :],
                        reads=("z1t",))
            m2, r2 = emit_ln_tile(p, z1t[:, :], "z1t", lntmp[:, :], lstat)
            t.op("vector", lambda e: e.tensor_scalar(
                out=z1n[:, :], in0=z1t[:, :], scalar1=m2, scalar2=r2,
                op0=ALU.subtract, op1=ALU.mult),
                reads=("z1t", "lnstat"), writes=("z1n",))
            for di in range(8):
                pb = di % 2
                t.op("tensor", lambda e, di=di, pb=pb: e.transpose(
                    pf[pb][:, :], z1n[:, di * 128:(di + 1) * 128],
                    identf[:, :]), reads=("z1n", "identf"), writes=(f"pf{pb}",))
                t.op("vector", lambda e, di=di, pb=pb: e.tensor_scalar(
                    out=lntmp[:, 0:128], in0=pf[pb][:, :],
                    scalar1=lng[:, di:di + 1], scalar2=lnb[:, di:di + 1],
                    op0=ALU.mult, op1=ALU.add),
                    reads=(f"pf{pb}", "lng", "lnb"), writes=(f"pf{pb}", "lntmp"))
                t.store(
                    bass.AP(o_z2.tensor, di * 128 * ST + gt, [[ST, 128], [1, 128]]),
                    lntmp[:, 0:128], reads=("lntmp",))
    return p.close()


def build_l4():
    p = Prog("l4")
    z2f = p.inp("z2f", [D, SH])
    wxt = p.inp("wxt", [16, D, 128])
    wct = p.inp("wct", [16, E, 128])
    wdt = p.inp("wdt", [16, E, 128])
    bxp = p.inp("bxp", [128, 16])
    b_c = p.inp("b_c", [128, 16])
    b_d = p.inp("b_d", [128, 16])
    convb = p.inp("convb", [128, 16])
    convw = p.inp("convw", [128, 64])
    ldl = p.inp("ldl", [128, 16])
    A_l = p.inp("A_l", [128, 256])
    Bm_l = p.inp("Bm_l", [128, 256])
    hmask = p.inp("hmask", [128, 1])
    o_g = p.out("o_g", [E, SC])
    o_ends = p.out("o_ends", [128, 16])
    t = p.t

    cvec = {}
    for nm, ap_, wdt_ in (("bxp", bxp, 16), ("b_c", b_c, 16), ("b_d", b_d, 16),
                          ("convb", convb, 16), ("convw", convw, 64),
                          ("ldl", ldl, 16)):
        cvec[nm] = p.sb("c_" + nm, [128, wdt_])
        t.load(cvec[nm][:, :], ap_, writes=(nm,))
    AB = p.sb("c_AB", [128, 256])
    t.load(AB[:, :], A_l, writes=("ABa",))
    BB = p.sb("c_BB", [128, 256])
    t.load(BB[:, :], Bm_l, writes=("ABb",))
    hm = p.sb("c_hm", [128, 1])
    t.load(hm[:, :], hmask, writes=("hmask",))
    dl = p.sb("c_dl", [128, 16 * 4])
    delta, a_, C_ = dl[:, 0:16], dl[:, 16:32], dl[:, 32:48]
    t.op("vector", lambda e: e.tensor_scalar(
        out=dl[:, 48:64], in0=cvec["ldl"][:, :], scalar1=0.0, scalar2=None,
        op0=ALU.max), reads=("ldl",), writes=("dl",))
    t.op("scalar", lambda e: e.activation(
        out=delta, in_=cvec["ldl"][:, :], func=AF.Abs),
        reads=("ldl",), writes=("dl",))
    t.op("scalar", lambda e: e.activation(
        out=a_, in_=delta, func=AF.Exp, scale=-1.0), reads=("dl",), writes=("dl",))
    t.op("vector", lambda e: e.tensor_scalar(
        out=a_, in0=a_, scalar1=1.0, scalar2=None, op0=ALU.add),
        reads=("dl",), writes=("dl",))
    t.op("scalar", lambda e: e.activation(out=C_, in_=a_, func=AF.Ln),
         reads=("dl",), writes=("dl",))
    t.op("vector", lambda e: e.tensor_tensor(
        out=delta, in0=dl[:, 48:64], in1=C_, op=ALU.add),
        reads=("dl",), writes=("dl",))
    t.op("scalar", lambda e: e.activation(
        out=a_, in_=delta, func=AF.Exp, scale=-1.0), reads=("dl",), writes=("dl",))
    t.op("vector", lambda e: e.tensor_tensor(
        out=AB[:, :], in0=AB[:, :], in1=BB[:, :], op=ALU.mult),
        reads=("ABa", "ABb"), writes=("ABa",))
    for tt in range(16):
        t.op("vector", lambda e, tt=tt: e.tensor_reduce(
            out=C_[:, tt:tt + 1], in_=AB[:, tt * 16:(tt + 1) * 16], axis=AXX,
            op=ALU.add), reads=("ABa",), writes=("dl",))

    Z2B = [p.sb(f"z2b{k}", [128, SH], BF16) for k in range(8)]
    ztmp = p.sb("ztmp", [128, SH])
    for k in range(8):
        t.load(ztmp[:, :], z2f[k * 128:(k + 1) * 128, :], writes=("ztmp",))
        t.op("vector", lambda e, k=k: e.tensor_copy(
            out=Z2B[k][:, :], in_=ztmp[:, :]), reads=("ztmp",), writes=(f"z2b{k}",))

    ph = [p.ps(f"ph{i}", [128, 512], F32) for i in range(2)]
    XPs = [p.sb(f"xps{i}", [128, SH], BF16) for i in range(2)]
    XC = [p.sb(f"xc{m}", [128, SC], BF16) for m in range(16)]
    wxf = p.sb("wxf", [128, 8 * 128])
    wxb = p.sb("wxb", [128, 8 * 128], BF16)
    cv = [p.sb(f"cv{i}", [128, SC], BF16) for i in range(2)]
    chunks = _nchunks(SH)
    for m in range(16):
        t.load(wxf[:, :], bass.AP(wxt.tensor, m * D * 128,
                                  [[128, 128], [128 * 128, 8], [1, 128]]),
               writes=("wxf",))
        t.op("vector", lambda e: e.tensor_copy(out=wxb[:, :], in_=wxf[:, :]),
             reads=("wxf",), writes=("wxb",))
        xk = f"xps{m % 2}"
        for ci, (n0, wd) in enumerate(chunks):
            b = ci % 2
            pk = f"ph{b}"
            for k in range(8):
                t.op("tensor", lambda e, k=k, b=b: e.matmul(
                    ph[b][:, 0:wd], wxb[:, k * 128:(k + 1) * 128],
                    Z2B[k][:, n0:n0 + wd], start=(k == 0), stop=(k == 7)),
                    reads=("wxb", f"z2b{k}"), writes=(pk,))
            t.op("vector", lambda e, b=b, m=m, n0=n0, wd=wd: e.tensor_scalar(
                out=XPs[m % 2][:, n0:n0 + wd], in0=ph[b][:, 0:wd],
                scalar1=cvec["bxp"][:, m:m + 1], scalar2=None, op0=ALU.add),
                reads=(pk, "bxp"), writes=(pk, xk))
        t.op("vector", lambda e: e.tensor_scalar(
            out=XPs[m % 2][:, 125:128], in0=XPs[m % 2][:, 125:128],
            scalar1=hm[:, 0:1], scalar2=None, op0=ALU.mult),
            reads=(xk, "hmask"), writes=(xk,))
        t.op("vector", lambda e: e.tensor_scalar(
            out=cv[0][:, :], in0=XPs[m % 2][:, 125:125 + SC],
            scalar1=cvec["convw"][:, 4 * m:4 * m + 1], scalar2=None,
            op0=ALU.mult), reads=(xk, "convw"), writes=("cv0",))
        for kk in range(1, 4):
            src, dst = (kk + 1) % 2, kk % 2
            t.op("vector", lambda e, kk=kk, src=src, dst=dst: e.scalar_tensor_tensor(
                out=cv[dst][:, :], in0=XPs[m % 2][:, 125 + kk:125 + kk + SC],
                scalar=cvec["convw"][:, 4 * m + kk:4 * m + kk + 1],
                in1=cv[src][:, :], op0=ALU.mult, op1=ALU.add),
                reads=(xk, "convw", f"cv{src}"), writes=(f"cv{dst}",))
        t.op("scalar", lambda e, m=m: e.activation(
            out=XC[m][:, :], in_=cv[1][:, :], func=AF.Silu,
            bias=cvec["convb"][:, m:m + 1], scale=1.0),
            reads=("cv1", "convb"), writes=(f"xc{m}",))

    wcf = p.sb("wcf", [128, 16 * 128])
    wcb = p.sb("wcb", [128, 16 * 128], BF16)
    wdf = p.sb("wdf", [128, 16 * 128])
    wdb = p.sb("wdb", [128, 16 * 128], BF16)
    ct = p.sb("ct", [128, 512])
    dt_ = p.sb("dt", [128, 512])
    ub = p.sb("ub", [128, SC], BF16)
    gb = p.sb("gb", [128, SC])
    gc = p.sb("gc", [128, SC])
    ends = p.sb("ends", [128, 16])
    for m in range(16):
        t.load(wcf[:, :], bass.AP(wct.tensor, m * E * 128,
                                  [[128, 128], [128 * 128, 16], [1, 128]]),
               writes=("wcf",))
        t.op("vector", lambda e: e.tensor_copy(out=wcb[:, :], in_=wcf[:, :]),
             reads=("wcf",), writes=("wcb",))
        t.load(wdf[:, :], bass.AP(wdt.tensor, m * E * 128,
                                  [[128, 128], [128 * 128, 16], [1, 128]]),
               writes=("wdf",))
        t.op("vector", lambda e: e.tensor_copy(out=wdb[:, :], in_=wdf[:, :]),
             reads=("wdf",), writes=("wdb",))
        for (n0, wd) in _nchunks(SC):
            for k in range(16):
                t.op("tensor", lambda e, k=k: e.matmul(
                    ph[0][:, 0:wd], wcb[:, k * 128:(k + 1) * 128],
                    XC[k][:, n0:n0 + wd], start=(k == 0), stop=(k == 15)),
                    reads=("wcb", f"xc{k}"), writes=("ph0",))
            for k in range(16):
                t.op("tensor", lambda e, k=k: e.matmul(
                    ph[1][:, 0:wd], wdb[:, k * 128:(k + 1) * 128],
                    XC[k][:, n0:n0 + wd], start=(k == 0), stop=(k == 15)),
                    reads=("wdb", f"xc{k}"), writes=("ph1",))
            t.op("vector", lambda e, m=m, wd=wd: e.tensor_scalar(
                out=ct[:, 0:wd], in0=ph[0][:, 0:wd],
                scalar1=cvec["b_c"][:, m:m + 1], scalar2=None, op0=ALU.add),
                reads=("ph0", "b_c"), writes=("ph0", "ct"))
            t.op("scalar", lambda e, m=m: e.activation(
                out=dt_[:, 0:wd], in_=ph[1][:, 0:wd], func=AF.Sigmoid,
                bias=cvec["b_d"][:, m:m + 1], scale=1.0),
                reads=("ph1", "b_d"), writes=("ph1", "dt"))
            t.op("vector", lambda e, n0=n0, wd=wd: e.tensor_tensor(
                out=ub[:, n0:n0 + wd], in0=ct[:, 0:wd], in1=dt_[:, 0:wd],
                op=ALU.mult), reads=("ct", "dt"), writes=("ub",))
        t.op("vector", lambda e, m=m: e.tensor_tensor_scan(
            out=gb[:, :], data0=bcast_col(a_[:, m:m + 1], SC), data1=ub[:, :],
            initial=0.0, op0=ALU.mult, op1=ALU.add),
            reads=("ub", "dl"), writes=("gb",))
        t.op("vector", lambda e, m=m: e.tensor_copy(
            out=ends[:, m:m + 1], in_=gb[:, SC - 1:SC]),
            reads=("gb",), writes=("ends",))
        t.op("vector", lambda e, m=m: e.tensor_scalar(
            out=gc[:, :], in0=gb[:, :], scalar1=C_[:, m:m + 1], scalar2=None,
            op0=ALU.mult), reads=("gb", "dl"), writes=("gc",))
        t.store(o_g[m * 128:(m + 1) * 128, :], gc[:, :], reads=("gc",))
    t.store(o_ends, ends[:, :], reads=("ends",))
    return p.close()


def build_l5():
    p = Prog("l5")
    i_g = p.inp("i_g", [E, SC])
    z2f = p.inp("z2f", [D, SH])
    i_z1 = p.inp("i_z1", [SC, D])
    i_xp = p.inp("i_xp", [SH, D])
    ends_all = p.inp("ends_all", [NCORES * 128, 16])
    kp = p.inp("kp", [1, 16])
    wgt = p.inp("wgt", [16, D, 128])
    wot = p.inp("wot", [E, D])
    bg = p.inp("bg", [128, 16])
    bomr = p.inp("bomr", [1, D])
    ldl = p.inp("ldl", [128, 16])
    A_l = p.inp("A_l", [128, 256])
    Bm_l = p.inp("Bm_l", [128, 256])
    o_xpp = p.out("o_xpp", [SC, D])
    o_a3 = p.out("o_a3", [1, 1])
    t = p.t

    ones, w_col = emit_witness(p)
    cbg = p.sb("c_bg", [128, 16])
    t.load(cbg[:, :], bg, writes=("bg",))
    cld = p.sb("c_ldl", [128, 16])
    t.load(cld[:, :], ldl, writes=("ldl",))
    AB = p.sb("c_AB", [128, 256])
    t.load(AB[:, :], A_l, writes=("ABa",))
    BB = p.sb("c_BB", [128, 256])
    t.load(BB[:, :], Bm_l, writes=("ABb",))
    bo = p.sb("borow", [1, D])
    t.load(bo[:, :], bomr, writes=("borow",))
    bob = p.sb("borowb", [1, D], BF16)
    t.op("vector", lambda e: e.tensor_copy(out=bob[:, :], in_=bo[:, :]),
         reads=("borow",), writes=("borowb",))
    onesb = p.sb("onesb", [1, 128], BF16)
    t.op("vector", lambda e: e.tensor_copy(out=onesb[:, :], in_=ones[:, :]),
         reads=("ones",), writes=("onesb",))
    dl = p.sb("c_dl", [128, 16 * 4])
    delta, a_, C_ = dl[:, 0:16], dl[:, 16:32], dl[:, 32:48]
    t.op("vector", lambda e: e.tensor_scalar(
        out=dl[:, 48:64], in0=cld[:, :], scalar1=0.0, scalar2=None,
        op0=ALU.max), reads=("ldl",), writes=("dl",))
    t.op("scalar", lambda e: e.activation(out=delta, in_=cld[:, :], func=AF.Abs),
         reads=("ldl",), writes=("dl",))
    t.op("scalar", lambda e: e.activation(
        out=a_, in_=delta, func=AF.Exp, scale=-1.0), reads=("dl",), writes=("dl",))
    t.op("vector", lambda e: e.tensor_scalar(
        out=a_, in0=a_, scalar1=1.0, scalar2=None, op0=ALU.add),
        reads=("dl",), writes=("dl",))
    t.op("scalar", lambda e: e.activation(out=C_, in_=a_, func=AF.Ln),
         reads=("dl",), writes=("dl",))
    t.op("vector", lambda e: e.tensor_tensor(
        out=delta, in0=dl[:, 48:64], in1=C_, op=ALU.add),
        reads=("dl",), writes=("dl",))
    t.op("scalar", lambda e: e.activation(
        out=a_, in_=delta, func=AF.Exp, scale=-1.0), reads=("dl",), writes=("dl",))
    t.op("vector", lambda e: e.tensor_tensor(
        out=AB[:, :], in0=AB[:, :], in1=BB[:, :], op=ALU.mult),
        reads=("ABa", "ABb"), writes=("ABa",))
    for tt in range(16):
        t.op("vector", lambda e, tt=tt: e.tensor_reduce(
            out=C_[:, tt:tt + 1], in_=AB[:, tt * 16:(tt + 1) * 16], axis=AXX,
            op=ALU.add), reads=("ABa",), writes=("dl",))

    kr = p.sb("kp_r", [1, 16])
    t.load(kr[:, :], kp, writes=("kp",))
    KB = p.sb("kb", [128, 16])
    psc = p.ps("pscal", [128, 16], F32)
    t.op("tensor", lambda e: e.matmul(
        psc[:, 0:16], ones[:, :], kr[0:1, 0:16], start=True, stop=True),
        reads=("ones", "kp"), writes=("pscal",))
    t.op("scalar", lambda e: e.activation(out=KB[:, :], in_=psc[:, 0:16],
                                          func=AF.Copy),
         reads=("pscal",), writes=("kb",))
    ej = p.sb("ej", [128, 16])
    cf = p.sb("cf", [128, 16])
    carry = p.sb("carry", [128, 16 * 2])
    t.op("vector", lambda e: e.memset(carry[:, 0:16], 0.0), writes=("carry",))
    for j in range(NCORES):
        t.load(ej[:, :], ends_all[j * 128:(j + 1) * 128, :], writes=("ej",))
        t.op("scalar", lambda e, j=j: e.activation(
            out=cf[:, :], in_=delta, func=AF.Exp, scale=KB[:, j:j + 1]),
            reads=("dl", "kb"), writes=("cf",))
        t.op("vector", lambda e, j=j: e.tensor_scalar(
            out=cf[:, :], in0=cf[:, :], scalar1=KB[:, 8 + j:9 + j], scalar2=None,
            op0=ALU.mult), reads=("cf", "kb"), writes=("cf",))
        t.op("vector", lambda e: e.tensor_tensor(
            out=cf[:, :], in0=cf[:, :], in1=ej[:, :], op=ALU.mult),
            reads=("cf", "ej"), writes=("cf",))
        t.op("vector", lambda e: e.tensor_tensor(
            out=carry[:, 0:16], in0=carry[:, 0:16], in1=cf[:, :], op=ALU.add),
            reads=("carry", "cf"), writes=("carry",))

    Z2B = [p.sb(f"z2b{k}", [128, SC], BF16) for k in range(8)]
    ztmp = p.sb("ztmp", [128, SC])
    for k in range(8):
        t.load(ztmp[:, :], z2f[k * 128:(k + 1) * 128, HALO:SH], writes=("ztmp",))
        t.op("vector", lambda e, k=k: e.tensor_copy(
            out=Z2B[k][:, :], in_=ztmp[:, :]), reads=("ztmp",), writes=(f"z2b{k}",))

    ph = [p.ps(f"ph{i}", [128, 512], F32) for i in range(2)]
    XG = [p.sb(f"xg{m}", [128, SC], BF16) for m in range(16)]
    zc = p.sb("zc", [128, 1])
    t.op("vector", lambda e: e.memset(zc[:, :], 0.0), writes=("zc",))
    crb = p.sb("crb", [128, SC])
    gbr = p.sb("gbr", [128, SC])
    t1b = p.sb("t1b", [128, SC])
    gs = p.sb("gs", [128, SC], BF16)
    wgf = p.sb("wgf", [128, 8 * 128])
    wgb = p.sb("wgb", [128, 8 * 128], BF16)
    for m in range(16):
        t.op("vector", lambda e, m=m: e.tensor_tensor_scan(
            out=crb[:, :], data0=bcast_col(a_[:, m:m + 1], SC),
            data1=bcast_col(zc[:, 0:1], SC),
            initial=carry[:, m:m + 1], op0=ALU.mult, op1=ALU.add),
            reads=("dl", "zc", "carry"), writes=("crb",))
        t.load(gbr[:, :], i_g[m * 128:(m + 1) * 128, :], writes=("gbr",))
        t.op("vector", lambda e, m=m: e.scalar_tensor_tensor(
            out=t1b[:, :], in0=crb[:, :], scalar=C_[:, m:m + 1], in1=gbr[:, :],
            op0=ALU.mult, op1=ALU.add), reads=("crb", "dl", "gbr"), writes=("t1b",))
        t.load(wgf[:, :], bass.AP(wgt.tensor, m * D * 128,
                                  [[128, 128], [128 * 128, 8], [1, 128]]),
               writes=("wgf",))
        t.op("vector", lambda e: e.tensor_copy(out=wgb[:, :], in_=wgf[:, :]),
             reads=("wgf",), writes=("wgb",))
        for ci, (n0, wd) in enumerate(_nchunks(SC)):
            b = ci % 2
            pk = f"ph{b}"
            for k in range(8):
                t.op("tensor", lambda e, k=k, b=b: e.matmul(
                    ph[b][:, 0:wd], wgb[:, k * 128:(k + 1) * 128],
                    Z2B[k][:, n0:n0 + wd], start=(k == 0), stop=(k == 7)),
                    reads=("wgb", f"z2b{k}"), writes=(pk,))
            t.op("scalar", lambda e, m=m, b=b, n0=n0, wd=wd: e.activation(
                out=gs[:, n0:n0 + wd], in_=ph[b][:, 0:wd], func=AF.Sigmoid,
                bias=cbg[:, m:m + 1], scale=1.0),
                reads=(pk, "bg"), writes=(pk, "gs"))
        t.op("vector", lambda e, m=m: e.tensor_tensor(
            out=XG[m][:, :], in0=t1b[:, :], in1=gs[:, :], op=ALU.mult),
            reads=("t1b", "gs"), writes=(f"xg{m}",))

    WOB = [p.sb(f"wob{k}", [128, D], BF16) for k in range(16)]
    wtmp = p.sb("wtmp", [128, D])
    for k in range(16):
        t.load(wtmp[:, :], wot[k * 128:(k + 1) * 128, :], writes=("wtmp",))
        t.op("vector", lambda e, k=k: e.tensor_copy(
            out=WOB[k][:, :], in_=wtmp[:, :]), reads=("wtmp",), writes=(f"wob{k}",))
    ow = p.sb("ow", [128, D])
    xt2 = p.sb("xt2", [128, D])
    z1t = p.sb("z1t", [128, D])
    yt = p.sb("yt", [128, D])
    lntmp = p.sb("clntmp", [128, D])
    lstat = p.sb("clstat", [128, 8])
    mx = p.sb("mx3", [128, 2])
    t.op("vector", lambda e: e.memset(mx[:, 0:1], 0.0), writes=("mx3",))
    for j in range(NT_R):
        for nn in range(2):
            pk = f"ph{nn}"
            for k in range(16):
                t.op("tensor", lambda e, k=k, nn=nn, j=j: e.matmul(
                    ph[nn][:, 0:512], XG[k][:, j * 128:(j + 1) * 128],
                    WOB[k][:, nn * 512:(nn + 1) * 512],
                    start=(k == 0), stop=False),
                    reads=(f"xg{k}", f"wob{k}"), writes=(pk,))
            t.op("tensor", lambda e, nn=nn: e.matmul(
                ph[nn][:, 0:512], onesb[0:1, :], bob[0:1, nn * 512:(nn + 1) * 512],
                start=False, stop=True), reads=("onesb", "borowb"), writes=(pk,))
            t.op("scalar", lambda e, nn=nn: e.activation(
                out=ow[:, nn * 512:(nn + 1) * 512], in_=ph[nn][:, 0:512],
                func=AF.Copy), reads=(pk,), writes=(pk, "ow"))
        t.load(xt2[:, :], i_xp[HALO + j * 128:HALO + (j + 1) * 128, :],
               writes=("xt2",))
        t.load(z1t[:, :], i_z1[j * 128:(j + 1) * 128, :], writes=("z1t",))
        t.op("vector", lambda e: e.tensor_tensor(
            out=ow[:, :], in0=ow[:, :], in1=xt2[:, :], op=ALU.add),
            reads=("ow", "xt2"), writes=("ow",))
        t.op("vector", lambda e: e.tensor_tensor(
            out=ow[:, :], in0=ow[:, :], in1=z1t[:, :], op=ALU.add),
            reads=("ow", "z1t"), writes=("ow",))
        t.store(o_xpp[j * 128:(j + 1) * 128, :], ow[:, :], reads=("ow",))
        m1, r1 = emit_ln_tile(p, ow[:, :], "ow", lntmp[:, :], lstat)
        t.op("vector", lambda e: e.tensor_scalar(
            out=yt[:, :], in0=ow[:, :], scalar1=m1, scalar2=r1,
            op0=ALU.subtract, op1=ALU.mult), reads=("ow", "lnstat"), writes=("yt",))
        t.op("vector", lambda e: e.tensor_reduce(
            out=mx[:, 1:2], in_=yt[:, :], axis=AXX, op=ALU.max,
            apply_absolute_value=True), reads=("yt",), writes=("mx3",))
        t.op("vector", lambda e: e.tensor_tensor(
            out=mx[:, 0:1], in0=mx[:, 0:1], in1=mx[:, 1:2], op=ALU.max),
            reads=("mx3",), writes=("mx3",))
    emit_amax_scalar(p, mx[:, 0:1], "mx3", o_a3, "scr3")
    return p.close()



def _prep_weights(iv):
    f32 = np.float32
    w = {}
    w["w1t"] = np.ascontiguousarray(np.asarray(iv["L0_ffn_W_in"], f32).T)
    w["w1o"] = np.ascontiguousarray(np.asarray(iv["L0_ffn_W_out"], f32))
    win = np.asarray(iv["L0_attn_W_in"], f32)
    w["wxt_t"] = np.ascontiguousarray(
        win[:E].T.reshape(D, 16, 128).transpose(1, 0, 2))
    w["wgt_t"] = np.ascontiguousarray(
        win[E:].T.reshape(D, 16, 128).transpose(1, 0, 2))
    w["wct_t"] = np.ascontiguousarray(
        np.asarray(iv["L0_attn_W_c"], f32).T.reshape(E, 16, 128).transpose(1, 0, 2))
    w["wdt_t"] = np.ascontiguousarray(
        np.asarray(iv["L0_attn_W_d"], f32).T.reshape(E, 16, 128).transpose(1, 0, 2))
    w["wot"] = np.ascontiguousarray(np.asarray(iv["L0_attn_W_out"], f32).T)
    w["b_out_row"] = np.ascontiguousarray(
        np.asarray(iv["L0_ffn_b_out"], f32).reshape(1, D))
    w["bom_row"] = np.ascontiguousarray(
        np.asarray(iv["L0_attn_b_out"], f32).reshape(1, D))

    def vec(v, n):
        return np.ascontiguousarray(np.asarray(v, f32).reshape(n, 128).T)

    w["b_in_f"] = vec(iv["L0_ffn_b_in"], 32)
    w["b_out_f"] = vec(iv["L0_ffn_b_out"], 8)
    w["ln_g"] = vec(iv["L0_attn_ln_g"], 8)
    w["ln_b"] = vec(iv["L0_attn_ln_b"], 8)
    bi = np.asarray(iv["L0_attn_b_in"], f32)
    w["bxp"] = vec(bi[:E], 16)
    w["bg"] = vec(bi[E:], 16)
    w["b_c"] = vec(iv["L0_attn_b_c"], 16)
    w["b_d"] = vec(iv["L0_attn_b_d"], 16)
    w["convb"] = vec(iv["L0_attn_conv_b"], 16)
    w["ld"] = vec(iv["L0_attn_log_delta"], 16)
    w["bom"] = vec(iv["L0_attn_b_out"], 8)
    cw = np.asarray(iv["L0_attn_conv_w"], f32)[:, 0, :]
    w["convw"] = np.ascontiguousarray(
        cw.reshape(16, 128, 4).transpose(1, 0, 2).reshape(128, 64))
    w["A_l"] = np.ascontiguousarray(
        np.asarray(iv["L0_attn_A"], f32).reshape(16, 128, 16)
        .transpose(1, 0, 2).reshape(128, 256))
    w["Bm_l"] = np.ascontiguousarray(
        np.asarray(iv["L0_attn_Bm"], f32).reshape(16, 128, 16)
        .transpose(1, 0, 2).reshape(128, 256))
    return w


def _shard_x(x):
    x2 = np.ascontiguousarray(np.asarray(x, np.float32).reshape(S, D))
    xs = []
    for c in range(NCORES):
        lo = c * SC - HALO
        if lo < 0:
            sl = np.concatenate(
                [np.zeros((HALO, D), np.float32), x2[0:(c + 1) * SC]], axis=0)
        else:
            sl = x2[lo:(c + 1) * SC]
        xs.append(np.ascontiguousarray(sl))
    return xs


def _gather_scalar(res, key):
    return np.ascontiguousarray(
        np.concatenate([np.asarray(res.results[c][key]).reshape(1, 1)
                        for c in range(NCORES)], axis=1))


_NC_CACHE = {}


def _run(tag, build_fn, in_maps):
    if tag not in _NC_CACHE:
        _NC_CACHE[tag] = build_fn()
    return run_bass_kernel_spmd(_NC_CACHE[tag], in_maps, core_ids=list(range(NCORES)))


def _kp_rows():
    rows = []
    for c in range(NCORES):
        kpow = [(-2048.0 * (c - 1 - j)) if j < c else 0.0 for j in range(NCORES)]
        mask = [1.0 if j < c else 0.0 for j in range(NCORES)]
        rows.append(np.ascontiguousarray(
            np.array([kpow + mask], dtype=np.float32)))
    return rows


def kernel(**inputs):
    w = _prep_weights(inputs)
    xsh = _shard_x(inputs["L0_attn_x"])

    r1 = _run("l1", build_l1, [dict(xs=xsh[c]) for c in range(NCORES)])
    am1 = _gather_scalar(r1, "o_a1")

    r2 = _run("l2", lambda: build_l2(NT_H, 1, "l2"),
              [dict(xs=xsh[c], am1=am1, w1t=w["w1t"], b_in_f=w["b_in_f"])
               for c in range(NCORES)])
    am2 = _gather_scalar(r2, "o_a2")

    r3 = _run("l3", lambda: build_ffn_out("l3", NT_H, True),
              [dict(xs=xsh[c], i_h=np.ascontiguousarray(r2.results[c]["o_h"]),
                    am2=am2, w1o=w["w1o"], b_out_row=w["b_out_row"],
                    ln_g=w["ln_g"], ln_b=w["ln_b"]) for c in range(NCORES)])

    r4 = _run("l4", build_l4,
              [dict(z2f=np.ascontiguousarray(r3.results[c]["o_z2"]),
                    wxt=w["wxt_t"], wct=w["wct_t"], wdt=w["wdt_t"],
                    bxp=w["bxp"], b_c=w["b_c"], b_d=w["b_d"],
                    convb=w["convb"], convw=w["convw"], ldl=w["ld"],
                    A_l=w["A_l"], Bm_l=w["Bm_l"],
                    hmask=np.full((128, 1), 0.0 if c == 0 else 1.0, np.float32))
               for c in range(NCORES)])
    ends_all = np.ascontiguousarray(np.concatenate(
        [np.asarray(r4.results[c]["o_ends"]) for c in range(NCORES)], axis=0))
    kps = _kp_rows()

    r5 = _run("l5", build_l5,
              [dict(i_g=np.ascontiguousarray(r4.results[c]["o_g"]),
                    z2f=np.ascontiguousarray(r3.results[c]["o_z2"]),
                    i_z1=np.ascontiguousarray(r3.results[c]["o_z1"]),
                    i_xp=np.ascontiguousarray(r3.results[c]["o_xp"]),
                    ends_all=ends_all, kp=kps[c], wgt=w["wgt_t"], wot=w["wot"],
                    bg=w["bg"], bomr=w["bom_row"], ldl=w["ld"],
                    A_l=w["A_l"], Bm_l=w["Bm_l"]) for c in range(NCORES)])
    am3 = _gather_scalar(r5, "o_a3")
    xpp = [np.ascontiguousarray(r5.results[c]["o_xpp"]) for c in range(NCORES)]

    r6 = _run("l6", lambda: build_l2(NT_R, 0, "l6"),
              [dict(xs=xpp[c], am1=am3, w1t=w["w1t"], b_in_f=w["b_in_f"])
               for c in range(NCORES)])
    am4 = _gather_scalar(r6, "o_a2")

    r7 = _run("l7", lambda: build_ffn_out("l7", NT_R, False),
              [dict(xs=xpp[c], i_h=np.ascontiguousarray(r6.results[c]["o_h"]),
                    am2=am4, w1o=w["w1o"], b_out_row=w["b_out_row"])
               for c in range(NCORES)])
    out = np.concatenate([np.asarray(r7.results[c]["o_out"])
                          for c in range(NCORES)], axis=0)
    return out.reshape(1, S, D).astype(np.float32, copy=False)


# version 13
# speedup vs baseline: 1.0142x; 1.0142x over previous
from contextlib import ExitStack

import numpy as np
import ml_dtypes

import concourse.bass as bass
import concourse.mybir as mybir
from concourse.bass_utils import run_bass_kernel_spmd

F32 = mybir.dt.float32
BF16 = mybir.dt.bfloat16
AF = mybir.ActivationFunctionType
ALU = mybir.AluOpType
AXX = mybir.AxisListType.X

NCORES = 8
S = 16384
SC = 2048
HALO = 128
SH = SC + HALO
D = 1024
E = 2048
DFF = 4096
NT_H = SH // 128
NT_R = SC // 128
EPS = 1e-5

C32 = 12582912.0
C81 = 1.5 * 2.0 ** 81
P29 = 2.0 ** 29
IP29 = 2.0 ** -29
INV127 = 1.0 / 127.0
INV32767 = 1.0 / 32767.0


def _nchunks(total, step=512):
    out, o = [], 0
    while o < total:
        w = min(step, total - o)
        out.append((o, w))
        o += w
    return out


class Trk:

    ENGS = ("sync", "tensor", "vector", "scalar", "gpsimd")

    def __init__(self, nc, ctx):
        self.nc = nc
        self.ctx = ctx
        self.sem = {e: ctx.enter_context(nc.semaphore("S_" + e)) for e in self.ENGS}
        self.cnt = {e: 0 for e in self.ENGS}
        self.waited = {e: {} for e in self.ENGS}
        self.bufs = {}
        self._dma_sems = {}

    def _dma_sem(self, key):
        sk = "D_" + str(key)
        if sk not in self.sem:
            self.sem[sk] = self.ctx.enter_context(self.nc.semaphore(sk))
            self.cnt[sk] = 0
        return sk

    def op(self, eng, fn, reads=(), writes=(), dma=False):
        sk = self._dma_sem(writes[0] if writes else "st") if dma else eng
        deps = []
        for k in reads:
            b = self.bufs.get(k)
            if b and b[0] is not None:
                deps.append(b[0])
        for k in writes:
            b = self.bufs.get(k)
            if b:
                if b[0] is not None:
                    deps.append(b[0])
                deps.extend(b[1].items())
        need = {}
        for se, c in deps:
            if se == eng == "tensor":
                continue
            if se.startswith("D_"):
                c = self.cnt[se]
            if c > self.waited[eng].get(se, 0):
                need[se] = max(need.get(se, 0), c)
        e = getattr(self.nc, eng)
        for se in sorted(need):
            e.wait_ge(self.sem[se], need[se])
            self.waited[eng][se] = need[se]
        ins = fn(e)
        inc = 16 if dma else 1
        ins.then_inc(self.sem[sk], inc)
        self.cnt[sk] += inc
        c = self.cnt[sk]
        for k in writes:
            self.bufs[k] = [(sk, c), {}]
        for k in reads:
            b = self.bufs.setdefault(k, [None, {}])
            b[1][sk] = c
        return ins

    def load(self, out_ap, in_ap, writes, reads=(), eng="sync"):
        return self.op(eng, lambda e: e.dma_start(out=out_ap, in_=in_ap),
                       reads=reads, writes=writes, dma=True)

    def store(self, out_ap, in_ap, reads, writes=(), eng="gpsimd"):
        return self.op(eng, lambda e: e.dma_start(out=out_ap, in_=in_ap),
                       reads=reads, writes=writes, dma=True)

    def finish(self):
        for eng in ("sync", "gpsimd"):
            e = getattr(self.nc, eng)
            for se in self.cnt:
                if self.cnt[se] > self.waited[eng].get(se, 0):
                    e.wait_ge(self.sem[se], self.cnt[se])
                    self.waited[eng][se] = self.cnt[se]


class Prog:
    def __init__(self, name):
        self.nc = bass.Bass("TRN2", target_bir_lowering=False)
        self.ctx = ExitStack()
        self.t = Trk(self.nc, self.ctx)
        self.name = name

    def inp(self, name, shape, dt=F32):
        return self.nc.dram_tensor(name, list(shape), dt, kind="ExternalInput").ap()

    def out(self, name, shape, dt=F32):
        return self.nc.dram_tensor(name, list(shape), dt, kind="ExternalOutput").ap()

    def scratch(self, name, shape, dt=F32):
        return self.nc.dram_tensor(name, list(shape), dt, kind="Internal").ap()

    def sb(self, name, shape, dt=F32, stack=None, side=None):
        return (stack or self.ctx).enter_context(
            self.nc.sbuf_tensor(name, list(shape), dt, side=side))

    def ps(self, name, shape, dt=F32):
        return self.ctx.enter_context(self.nc.psum_tensor(name, list(shape), dt))

    def barrier(self):
        t = self.t
        for eng in Trk.ENGS:
            e = getattr(self.nc, eng)
            for se in t.cnt:
                if t.cnt[se] > t.waited[eng].get(se, 0):
                    e.wait_ge(t.sem[se], t.cnt[se])
                    t.waited[eng][se] = t.cnt[se]

    def scope(self):
        return ExitStack()

    def close(self):
        self.t.finish()
        self.ctx.close()
        return self.nc


def bcast_col(col_ap, n):
    p0 = list(col_ap.ap[0])
    return bass.AP(col_ap.tensor, col_ap.offset, [p0, [0, n]])



def emit_identity(p, dt=BF16, key="ident"):
    t = p.t
    ident = p.sb(key, [128, 128], dt)
    t.op("gpsimd", lambda e: e.memset(ident[:, :], 0.0), writes=(key,))
    t.op("gpsimd", lambda e: e.affine_select(
        out=ident[:, :], in_=ident[:, :], compare_op=ALU.not_equal,
        fill=1.0, base=0, pattern=[[-1, 128]], channel_multiplier=1),
        reads=(key,), writes=(key,))
    return ident


def emit_wbase(p, ones_row, w_col, psc):
    t = p.t
    dst = p.sb("wbase", [128, 2], F32)
    t.op("vector", lambda e: e.tensor_scalar(
        out=p._wrow[:, 1:2], in0=w_col, scalar1=-1.0, scalar2=1.0,
        op0=ALU.mult, op1=ALU.add), reads=("wit",), writes=("wit",))
    t.op("vector", lambda e: e.tensor_scalar(
        out=p._wrow[:, 0:1], in0=w_col, scalar1=IP29, scalar2=None,
        op0=ALU.mult), reads=("wit",), writes=("wit",))
    t.op("tensor", lambda e: e.matmul(
        psc[:, 0:2], ones_row[:, :], p._wrow[0:1, 0:2], start=True, stop=True),
        reads=("ones", "wit"), writes=("pscal",))
    t.op("scalar", lambda e: e.activation(out=dst[:, :], in_=psc[:, 0:2],
                                          func=AF.Copy),
         reads=("pscal",), writes=("wit_b",))
    return dst[:, 1:2], dst[:, 0:1]


def emit_witness(p):
    t = p.t
    ones = p.sb("ones_row", [1, 128], F32)
    t.op("vector", lambda e: e.memset(ones[:, :], 1.0), writes=("ones",))
    wt = p.sb("wit", [1, 2], F32)
    t.op("vector", lambda e: e.memset(wt[:, 0:1], 0.75), writes=("wit",))
    t.op("vector", lambda e: e.tensor_scalar(
        out=wt[:, 1:2], in0=wt[:, 0:1], scalar1=P29, scalar2=C81,
        op0=ALU.mult, op1=ALU.add), reads=("wit",), writes=("wit",))
    t.op("vector", lambda e: e.tensor_scalar(
        out=wt[:, 0:1], in0=wt[:, 1:2], scalar1=C81, scalar2=IP29,
        op0=ALU.subtract, op1=ALU.mult), reads=("wit",), writes=("wit",))
    p._wrow = p.sb("wrow", [1, 2], F32)
    return ones, wt[:, 0:1]


def emit_ln_tile(p, x_ap, x_key, tmp, stat, sfx=""):
    t = p.t
    ks, kt = "lnstat" + sfx, "lntmp" + sfx
    m, r = stat[:, 0:1], stat[:, 1:2]
    t.op("vector", lambda e: e.tensor_reduce(
        out=stat[:, 4:5], in_=x_ap, axis=AXX, op=ALU.add),
        reads=(x_key,), writes=(ks,))
    t.op("vector", lambda e: e.tensor_scalar(
        out=m, in0=stat[:, 4:5], scalar1=1.0 / D, scalar2=None, op0=ALU.mult),
        reads=(ks,), writes=(ks,))
    t.op("vector", lambda e: e.scalar_tensor_tensor(
        out=tmp, in0=x_ap, scalar=0.0, in1=x_ap, op0=ALU.add, op1=ALU.mult,
        accum_out=stat[:, 2:3]), reads=(x_key,), writes=(kt, ks))
    t.op("vector", lambda e: e.tensor_tensor(
        out=stat[:, 5:6], in0=m, in1=m, op=ALU.mult),
        reads=(ks,), writes=(ks,))
    t.op("vector", lambda e: e.tensor_scalar(
        out=stat[:, 3:4], in0=stat[:, 2:3], scalar1=1.0 / D, scalar2=EPS,
        op0=ALU.mult, op1=ALU.add), reads=(ks,), writes=(ks,))
    t.op("vector", lambda e: e.tensor_tensor(
        out=stat[:, 3:4], in0=stat[:, 3:4], in1=stat[:, 5:6], op=ALU.subtract),
        reads=(ks,), writes=(ks,))
    t.op("scalar", lambda e: e.activation(
        out=stat[:, 6:7], in_=stat[:, 3:4], func=AF.Ln),
        reads=(ks,), writes=(ks,))
    t.op("scalar", lambda e: e.activation(
        out=r, in_=stat[:, 6:7], func=AF.Exp, scale=-0.5),
        reads=(ks,), writes=(ks,))
    return m, r


def emit_quant(p, src_ap, src_key, dst_ap, dst_key, scale_col, omwrc_col,
               w29rc_col, tmp1, tmp2, extra_reads=(), skip_scale=False):
    t = p.t
    er = tuple(extra_reads)
    if not skip_scale:
        t.op("vector", lambda e: e.tensor_scalar(
            out=tmp1, in0=src_ap, scalar1=scale_col, scalar2=None, op0=ALU.mult),
            reads=(src_key, "wit_b") + er, writes=("qt1",))
    t.op("vector", lambda e: e.tensor_scalar(
        out=tmp2, in0=tmp1, scalar1=C32, scalar2=-C32, op0=ALU.add, op1=ALU.add),
        reads=("qt1",), writes=("qt2",))
    t.op("vector", lambda e: e.tensor_scalar(
        out=tmp1, in0=tmp1, scalar1=P29, scalar2=C81, op0=ALU.mult, op1=ALU.add),
        reads=("qt1",), writes=("qt1",))
    t.op("vector", lambda e: e.tensor_scalar(
        out=tmp1, in0=tmp1, scalar1=C81, scalar2=w29rc_col, op0=ALU.subtract,
        op1=ALU.mult), reads=("qt1", "wit_b") + er, writes=("qt1",))
    t.op("vector", lambda e: e.scalar_tensor_tensor(
        out=dst_ap, in0=tmp2, scalar=omwrc_col, in1=tmp1, op0=ALU.mult,
        op1=ALU.add), reads=("qt1", "qt2", "wit_b") + er, writes=(dst_key,))


def emit_transpose(p, src_ap, src_key, pbank, pkey, dst_ap, dst_key,
                   ident, ident_key="ident", act_kwargs=None):
    t = p.t
    t.op("tensor", lambda e: e.transpose(pbank, src_ap, ident[:, :]),
         reads=(src_key, ident_key), writes=(pkey,))
    kw = act_kwargs or {}
    t.op("scalar", lambda e: e.activation(out=dst_ap, in_=pbank, func=AF.Copy, **kw),
         reads=(), writes=(pkey, dst_key))


def emit_amax_scalar(p, mx_col, mx_key, o_ap, scrname):
    t = p.t
    scr = p.scratch(scrname, [128, 1], F32)
    row = p.sb(scrname + "_row", [1, 130], F32)
    t.store(scr, mx_col, reads=(mx_key,), writes=(scrname + "_d",), eng="gpsimd")
    t.load(row[0:1, 0:128], scr.rearrange("a b -> b a"),
           writes=(scrname,), reads=(scrname + "_d",), eng="gpsimd")
    t.op("vector", lambda e: e.tensor_reduce(
        out=row[0:1, 128:129], in_=row[0:1, 0:128], axis=AXX, op=ALU.max),
        reads=(scrname,), writes=(scrname,))
    t.store(o_ap, row[0:1, 128:129], reads=(scrname,), eng="gpsimd")


def emit_scales(p, am_ap, am_name, ones_row, w_col, psc):
    t = p.t
    row = p.sb(am_name + "_r", [1, 16], F32)
    t.load(row[0:1, 0:8], am_ap, writes=(am_name,))
    t.op("vector", lambda e: e.tensor_reduce(
        out=row[0:1, 8:9], in_=row[0:1, 0:8], axis=AXX, op=ALU.max),
        reads=(am_name,), writes=(am_name,))
    t.op("vector", lambda e: e.tensor_scalar(
        out=row[0:1, 9:10], in0=row[0:1, 8:9], scalar1=INV32767, scalar2=None,
        op0=ALU.mult), reads=(am_name,), writes=(am_name,))
    t.op("vector", lambda e: e.reciprocal(
        out=row[0:1, 10:11], in_=row[0:1, 9:10]),
        reads=(am_name,), writes=(am_name,))
    t.op("vector", lambda e: e.tensor_copy(out=row[0:1, 11:12], in_=w_col),
         reads=("wit",), writes=(am_name,))
    dst = p.sb(am_name + "_b", [128, 5], F32)
    t.op("tensor", lambda e: e.matmul(
        psc[:, 0:3], ones_row[:, :], row[0:1, 9:12], start=True, stop=True),
        reads=("ones", am_name), writes=("pscal",))
    t.op("scalar", lambda e: e.activation(out=dst[:, 0:3], in_=psc[:, 0:3],
                                          func=AF.Copy),
         reads=("pscal",), writes=("wit_b",))
    t.op("vector", lambda e: e.tensor_scalar(
        out=dst[:, 3:4], in0=dst[:, 2:3], scalar1=-1.0, scalar2=1.0,
        op0=ALU.mult, op1=ALU.add), reads=("wit_b",), writes=("wit_b",))
    t.op("vector", lambda e: e.tensor_tensor(
        out=dst[:, 3:4], in0=dst[:, 3:4], in1=dst[:, 0:1], op=ALU.mult),
        reads=("wit_b",), writes=("wit_b",))
    t.op("vector", lambda e: e.tensor_scalar(
        out=dst[:, 4:5], in0=dst[:, 2:3], scalar1=IP29, scalar2=None,
        op0=ALU.mult), reads=("wit_b",), writes=("wit_b",))
    t.op("vector", lambda e: e.tensor_tensor(
        out=dst[:, 4:5], in0=dst[:, 4:5], in1=dst[:, 0:1], op=ALU.mult),
        reads=("wit_b",), writes=("wit_b",))
    return dst[:, 1:2], dst[:, 3:4], dst[:, 4:5]


def emit_yq(p, xs, sc, omwrc, w29rc, ident, yq, ntiles, pT, stack=None):
    t = p.t
    xt = [p.sb(f"qx{i}", [128, D], stack=stack) for i in range(2)]
    tmp = p.sb("qtmp", [128, D], stack=stack)
    t1 = p.sb("qtmp1", [128, D], stack=stack)
    t2 = p.sb("qtmp2", [128, D], stack=stack)
    yqt = p.sb("yqt", [128, D], BF16, stack=stack)
    stat = p.sb("qstat", [128, 8], stack=stack)
    for i in range(ntiles):
        b = i % 2
        xk = f"qx{b}"
        t.load(xt[b][:, :], xs[i * 128:(i + 1) * 128, :], writes=(xk,))
        m, r = emit_ln_tile(p, xt[b][:, :], xk, tmp[:, :], stat)
        t.op("vector", lambda e: e.tensor_tensor(
            out=stat[:, 7:8], in0=r, in1=sc, op=ALU.mult),
            reads=("lnstat", "wit_b"), writes=("lnstat",))
        t.op("vector", lambda e: e.tensor_scalar(
            out=t1[:, :], in0=xt[b][:, :], scalar1=m, scalar2=stat[:, 7:8],
            op0=ALU.subtract, op1=ALU.mult),
            reads=(xk, "lnstat"), writes=("qt1",))
        emit_quant(p, None, None, yqt[:, :], "yqt", sc, omwrc, w29rc,
                   t1[:, :], t2[:, :], skip_scale=True)
        for k in range(8):
            pb = k % 2
            emit_transpose(p, yqt[:, k * 128:(k + 1) * 128], "yqt",
                           pT[pb][:, :], f"pT{pb}",
                           yq[k][:, i * 128:(i + 1) * 128], f"yq{k}", ident)


def emit_w1t_quant(p, w1t, omwn, w29n, ident, sink, pT, stack=None):
    t = p.t
    wtmp = p.sb("wtmp", [128, D], stack=stack)
    wa = p.sb("wa", [128, 6], stack=stack)
    wt1 = p.sb("wq1", [128, D], stack=stack)
    wt2 = p.sb("wq2", [128, D], stack=stack)
    wq = p.sb("wq", [128, D], BF16, stack=stack)
    pW = pT
    for t32 in range(DFF // 128):
        t.load(wtmp[:, :], w1t[t32 * 128:(t32 + 1) * 128, :], writes=("wtmp",))
        t.op("vector", lambda e: e.tensor_reduce(
            out=wa[:, 0:1], in_=wtmp[:, :], axis=AXX, op=ALU.max,
            apply_absolute_value=True), reads=("wtmp",), writes=("wa",))
        t.op("vector", lambda e: e.tensor_scalar(
            out=wa[:, 1:2], in0=wa[:, 0:1], scalar1=INV127, scalar2=None,
            op0=ALU.mult), reads=("wa",), writes=("wa",))
        t.op("vector", lambda e: e.reciprocal(out=wa[:, 2:3], in_=wa[:, 1:2]),
             reads=("wa",), writes=("wa",))
        t.op("vector", lambda e: e.tensor_tensor(
            out=wa[:, 3:4], in0=omwn, in1=wa[:, 1:2], op=ALU.mult),
            reads=("wa", "wit_b"), writes=("wa",))
        t.op("vector", lambda e: e.tensor_tensor(
            out=wa[:, 4:5], in0=w29n, in1=wa[:, 1:2], op=ALU.mult),
            reads=("wa", "wit_b"), writes=("wa",))
        emit_quant(p, wtmp[:, :], "wtmp", wq[:, :], "wq", wa[:, 2:3],
                   wa[:, 3:4], wa[:, 4:5], wt1[:, :], wt2[:, :],
                   extra_reads=("wa",))
        for k in range(8):
            pb = k % 2
            dst_ap, dst_key = sink(k, t32)
            emit_transpose(p, wq[:, k * 128:(k + 1) * 128], "wq",
                           pW[pb][:, :], f"pT{pb}", dst_ap, dst_key, ident)


def emit_h_pass(p, yq, l1t, binf, ntiles, halo_tiles, hq_sink, ht_dt=BF16):
    t = p.t
    ht = [p.sb(f"ht{i}", [128, 512], ht_dt) for i in range(2)]
    pbk = [p.ps(f"ph{i}", [128, 512], F32) for i in range(2)]
    mx = p.sb("hmx", [128, 2])
    t.op("vector", lambda e: e.memset(mx[:, 0:1], 0.0), writes=("hmx",))
    chunks = _nchunks(ntiles * 128)
    for mi in range(DFF // 128):
        for ci, (n0, w) in enumerate(chunks):
            b = (mi * len(chunks) + ci) % 2
            pk = f"ph{b}"
            for k in range(8):
                t.op("tensor", lambda e, k=k: e.matmul(
                    pbk[b][:, 0:w], l1t[k][:, mi * 128:(mi + 1) * 128],
                    yq[k][:, n0:n0 + w], start=(k == 0), stop=(k == 7)),
                    reads=(f"l1t{k}", f"yq{k}"), writes=(pk,))
            t.op("scalar", lambda e: e.activation(
                out=ht[b][:, 0:w], in_=pbk[b][:, 0:w], func=AF.Gelu,
                bias=binf[:, mi:mi + 1], scale=1.0),
                reads=(pk, "b_in_f"), writes=(pk, f"ht{b}"))
            lo = max(0, halo_tiles * 128 - n0)
            if lo < w:
                t.op("vector", lambda e: e.tensor_reduce(
                    out=mx[:, 1:2], in_=ht[b][:, lo:w], axis=AXX, op=ALU.max,
                    apply_absolute_value=True), reads=(f"ht{b}",), writes=("hmx",))
                t.op("vector", lambda e: e.tensor_tensor(
                    out=mx[:, 0:1], in0=mx[:, 0:1], in1=mx[:, 1:2], op=ALU.max),
                    reads=("hmx",), writes=("hmx",))
            if hq_sink is not None:
                hq_sink(mi, n0, w, ht[b][:, 0:w], f"ht{b}")
    return mx[:, 0:1], "hmx"



def build_l1():
    p = Prog("l1")
    xs = p.inp("xs", [SH, D])
    o_a1 = p.out("o_a1", [1, 1])
    xt = [p.sb(f"xt{i}", [128, D]) for i in range(2)]
    tmp = [p.sb(f"tmp{i}", [128, D]) for i in range(2)]
    y = [p.sb(f"y{i}", [128, D]) for i in range(2)]
    stat = [p.sb(f"stat{i}", [128, 8]) for i in range(2)]
    mx = p.sb("mx", [128, 5])
    t = p.t
    t.op("vector", lambda e: e.memset(mx[:, 0:2], 0.0), writes=("mxp0", "mxp1"))
    for i in range(NT_H):
        b = i % 2
        xk = f"xt{b}"
        t.load(xt[b][:, :], xs[i * 128:(i + 1) * 128, :], writes=(xk,))
        m, r = emit_ln_tile(p, xt[b][:, :], xk, tmp[b][:, :], stat[b], sfx=str(b))
        t.op("vector", lambda e: e.tensor_scalar(
            out=y[b][:, :], in0=xt[b][:, :], scalar1=m, scalar2=r,
            op0=ALU.subtract, op1=ALU.mult),
            reads=(xk, "lnstat" + str(b)), writes=(f"y{b}",))
        t.op("vector", lambda e, b=b: e.tensor_reduce(
            out=mx[:, 3 + b:4 + b], in_=y[b][:, :], axis=AXX, op=ALU.max,
            apply_absolute_value=True), reads=(f"y{b}",), writes=(f"mxt{b}",))
        t.op("vector", lambda e, b=b: e.tensor_tensor(
            out=mx[:, b:b + 1], in0=mx[:, b:b + 1], in1=mx[:, 3 + b:4 + b],
            op=ALU.max), reads=(f"mxt{b}", f"mxp{b}"), writes=(f"mxp{b}",))
    t.op("vector", lambda e: e.tensor_tensor(
        out=mx[:, 2:3], in0=mx[:, 0:1], in1=mx[:, 1:2], op=ALU.max),
        reads=("mxp0", "mxp1"), writes=("mx",))
    emit_amax_scalar(p, mx[:, 2:3], "mx", o_a1, "scr1")
    return p.close()


def build_l2(ntiles, halo_tiles, name):
    p = Prog(name)
    ST = ntiles * 128
    xs = p.inp("xs", [ST, D])
    am1 = p.inp("am1", [1, 8])
    w1t = p.inp("w1t", [DFF, D])
    b_in = p.inp("b_in_f", [128, DFF // 128])
    o_a2 = p.out("o_a2", [1, 1])
    o_h = p.out("o_h", [DFF, ST])
    t = p.t
    ident = emit_identity(p)
    ones, w_col = emit_witness(p)
    binf = p.sb("binf", [128, DFF // 128])
    t.load(binf[:, :], b_in, writes=("b_in_f",))
    psc = p.ps("pscal", [128, 8], F32)
    pT = [p.ps(f"pT{i}", [128, 128], BF16) for i in range(2)]
    omwn, w29n = emit_wbase(p, ones, w_col, psc)
    sc, omwrc, w29rc = emit_scales(p, am1, "am1", ones, w_col, psc)
    yq = [p.sb(f"yq{k}", [128, ntiles * 128], BF16) for k in range(8)]
    emit_yq(p, xs, sc, omwrc, w29rc, ident, yq, ntiles, pT)
    l1t = [p.sb(f"l1t{k}", [128, DFF], BF16) for k in range(8)]

    def sink(k, t32):
        return l1t[k][:, t32 * 128:(t32 + 1) * 128], f"l1t{k}"

    emit_w1t_quant(p, w1t, omwn, w29n, ident, sink, pT)

    def h_sink(mi, n0, wd, h_ap, h_key):
        p.t.store(
            bass.AP(o_h.tensor, mi * 128 * ST + n0, [[ST, 128], [1, wd]]),
            h_ap, reads=(h_key,))

    mxc, mxk = emit_h_pass(p, yq, l1t, binf, ntiles, halo_tiles, h_sink,
                           ht_dt=F32)
    emit_amax_scalar(p, mxc, mxk, o_a2, "scr2")
    return p.close()



def emit_w1o_quant_to_scratch(p, w1o, omwn, w29n, w1oq_sb, stack=None):
    t = p.t
    wtmp = p.sb("owtmp", [128, D], stack=stack)
    wa = p.sb("owa", [128, 6], stack=stack)
    wt1 = p.sb("owq1", [128, D], stack=stack)
    wt2 = p.sb("owq2", [128, D], stack=stack)
    for k32 in range(DFF // 128):
        t.load(wtmp[:, :], w1o[k32 * 128:(k32 + 1) * 128, :], writes=("owtmp",))
        t.op("vector", lambda e: e.tensor_reduce(
            out=wa[:, 0:1], in_=wtmp[:, :], axis=AXX, op=ALU.max,
            apply_absolute_value=True), reads=("owtmp",), writes=("owa",))
        t.op("vector", lambda e: e.tensor_scalar(
            out=wa[:, 1:2], in0=wa[:, 0:1], scalar1=INV127, scalar2=None,
            op0=ALU.mult), reads=("owa",), writes=("owa",))
        t.op("vector", lambda e: e.reciprocal(out=wa[:, 2:3], in_=wa[:, 1:2]),
             reads=("owa",), writes=("owa",))
        t.op("vector", lambda e: e.tensor_tensor(
            out=wa[:, 3:4], in0=omwn, in1=wa[:, 1:2], op=ALU.mult),
            reads=("owa", "wit_b"), writes=("owa",))
        t.op("vector", lambda e: e.tensor_tensor(
            out=wa[:, 4:5], in0=w29n, in1=wa[:, 1:2], op=ALU.mult),
            reads=("owa", "wit_b"), writes=("owa",))
        emit_quant(p, wtmp[:, :], "owtmp", w1oq_sb[k32][:, :], f"w1oq{k32}",
                   wa[:, 2:3], wa[:, 3:4], wa[:, 4:5], wt1[:, :], wt2[:, :],
                   extra_reads=("owa",))


def build_ffn_out(name, ntiles, mamba_prep):
    p = Prog(name)
    ST = ntiles * 128
    xs = p.inp("xs", [ST, D])
    i_h = p.inp("i_h", [DFF, ST])
    am2 = p.inp("am2", [1, 8])
    w1o = p.inp("w1o", [DFF, D])
    b_out_row = p.inp("b_out_row", [1, D])
    if mamba_prep:
        o_xp = p.out("o_xp", [ST, D])
        o_z1 = p.out("o_z1", [SC, D])
        o_z2 = p.out("o_z2", [D, ST])
        ln_g = p.inp("ln_g", [128, 8])
        ln_b = p.inp("ln_b", [128, 8])
    else:
        o_out = p.out("o_out", [ST, D])
    t = p.t
    identf = emit_identity(p, dt=F32, key="identf")
    ones, w_col = emit_witness(p)
    borow = p.sb("borow", [1, D])
    t.load(borow[:, :], b_out_row, writes=("borow",))
    borowb = p.sb("borowb", [1, D], BF16)
    t.op("vector", lambda e: e.tensor_copy(out=borowb[:, :], in_=borow[:, :]),
         reads=("borow",), writes=("borowb",))
    onesb = p.sb("onesb", [1, 128], BF16)
    t.op("vector", lambda e: e.tensor_copy(out=onesb[:, :], in_=ones[:, :]),
         reads=("ones",), writes=("onesb",))
    if mamba_prep:
        lng = p.sb("lng", [128, 8])
        lnb = p.sb("lnb", [128, 8])
        t.load(lng[:, :], ln_g, writes=("lng",))
        t.load(lnb[:, :], ln_b, writes=("lnb",))
    psc = p.ps("pscal", [128, 8], F32)
    omwn, w29n = emit_wbase(p, ones, w_col, psc)
    sc2, omwrc2, w29rc2 = emit_scales(p, am2, "am2", ones, w_col, psc)

    w1oq = [p.sb(f"w1oq{k}", [128, D], BF16) for k in range(DFF // 128)]
    sc_o = p.scope()
    emit_w1o_quant_to_scratch(p, w1o, omwn, w29n, w1oq, stack=sc_o)
    p.barrier()
    sc_o.close()

    HQ = [p.sb(f"hq{mi}", [128, 512], BF16) for mi in range(DFF // 128)]
    hldt = [p.sb(f"hld{i}", [128, 512], F32) for i in range(2)]
    ph = [p.ps(f"ph{i}", [128, 512], F32) for i in range(2)]
    pf = [p.ps(f"pf{i}", [128, 128], F32) for i in range(2)]
    qt1 = p.sb("cqt1", [128, 512])
    qt2 = p.sb("cqt2", [128, 512])
    ow = p.sb("ow", [128, D])
    xt2 = p.sb("xt2", [128, D])
    z1t = p.sb("z1t", [128, D])
    z1n = p.sb("z1n", [128, D])
    lntmp = p.sb("clntmp", [128, D])
    lstat = p.sb("clstat", [128, 8])
    chunks = _nchunks(ST)
    for ci, (n0, wd) in enumerate(chunks):
        for mi in range(DFF // 128):
            b = mi % 2
            lk = f"hld{b}"
            t.load(hldt[b][:, 0:wd],
                   bass.AP(i_h.tensor, mi * 128 * ST + n0, [[ST, 128], [1, wd]]),
                   writes=(lk,))
            emit_quant(p, hldt[b][:, 0:wd], lk, HQ[mi][:, 0:wd],
                       f"hq{mi}", sc2, omwrc2, w29rc2, qt1[:, 0:wd],
                       qt2[:, 0:wd])
        for j in range(wd // 128):
            for nn in range(2):
                pk = f"ph{nn}"
                for k32 in range(DFF // 128):
                    t.op("tensor", lambda e, k32=k32, nn=nn: e.matmul(
                        ph[nn][:, 0:512],
                        HQ[k32][:, j * 128:(j + 1) * 128],
                        w1oq[k32][:, nn * 512:(nn + 1) * 512],
                        start=(k32 == 0), stop=False),
                        reads=(f"hq{k32}", f"w1oq{k32}"), writes=(pk,))
                t.op("tensor", lambda e, nn=nn: e.matmul(
                    ph[nn][:, 0:512], onesb[0:1, :],
                    borowb[0:1, nn * 512:(nn + 1) * 512],
                    start=False, stop=True),
                    reads=("onesb", "borowb"), writes=(pk,))
                t.op("scalar", lambda e, nn=nn: e.activation(
                    out=ow[:, nn * 512:(nn + 1) * 512], in_=ph[nn][:, 0:512],
                    func=AF.Copy, scale=0.5),
                    reads=(pk,), writes=(pk, "ow"))
            gt = n0 + j * 128
            t.load(xt2[:, :], xs[gt:gt + 128, :], writes=("xt2",))
            t.op("vector", lambda e: e.tensor_tensor(
                out=ow[:, :], in0=ow[:, :], in1=xt2[:, :], op=ALU.add),
                reads=("ow", "xt2"), writes=("ow",))
            if not mamba_prep:
                t.store(o_out[gt:gt + 128, :], ow[:, :], reads=("ow",))
                continue
            t.store(o_xp[gt:gt + 128, :], ow[:, :], reads=("ow",))
            m1, r1 = emit_ln_tile(p, ow[:, :], "ow", lntmp[:, :], lstat)
            t.op("vector", lambda e: e.tensor_scalar(
                out=z1t[:, :], in0=ow[:, :], scalar1=m1, scalar2=r1,
                op0=ALU.subtract, op1=ALU.mult),
                reads=("ow", "lnstat"), writes=("z1t",))
            if gt >= HALO:
                t.store(o_z1[gt - HALO:gt - HALO + 128, :], z1t[:, :],
                        reads=("z1t",))
            m2, r2 = emit_ln_tile(p, z1t[:, :], "z1t", lntmp[:, :], lstat)
            t.op("vector", lambda e: e.tensor_scalar(
                out=z1n[:, :], in0=z1t[:, :], scalar1=m2, scalar2=r2,
                op0=ALU.subtract, op1=ALU.mult),
                reads=("z1t", "lnstat"), writes=("z1n",))
            for di in range(8):
                pb = di % 2
                t.op("tensor", lambda e, di=di, pb=pb: e.transpose(
                    pf[pb][:, :], z1n[:, di * 128:(di + 1) * 128],
                    identf[:, :]), reads=("z1n", "identf"), writes=(f"pf{pb}",))
                t.op("vector", lambda e, di=di, pb=pb: e.tensor_scalar(
                    out=lntmp[:, 0:128], in0=pf[pb][:, :],
                    scalar1=lng[:, di:di + 1], scalar2=lnb[:, di:di + 1],
                    op0=ALU.mult, op1=ALU.add),
                    reads=(f"pf{pb}", "lng", "lnb"), writes=(f"pf{pb}", "lntmp"))
                t.store(
                    bass.AP(o_z2.tensor, di * 128 * ST + gt, [[ST, 128], [1, 128]]),
                    lntmp[:, 0:128], reads=("lntmp",))
    return p.close()


def build_l4():
    p = Prog("l4")
    z2f = p.inp("z2f", [D, SH])
    wxt = p.inp("wxt", [16, D, 128])
    wct = p.inp("wct", [16, E, 128])
    wdt = p.inp("wdt", [16, E, 128])
    bxp = p.inp("bxp", [128, 16])
    b_c = p.inp("b_c", [128, 16])
    b_d = p.inp("b_d", [128, 16])
    convb = p.inp("convb", [128, 16])
    convw = p.inp("convw", [128, 64])
    ldl = p.inp("ldl", [128, 16])
    A_l = p.inp("A_l", [128, 256])
    Bm_l = p.inp("Bm_l", [128, 256])
    hmask = p.inp("hmask", [128, 1])
    o_g = p.out("o_g", [E, SC])
    o_ends = p.out("o_ends", [128, 16])
    t = p.t

    cvec = {}
    for nm, ap_, wdt_ in (("bxp", bxp, 16), ("b_c", b_c, 16), ("b_d", b_d, 16),
                          ("convb", convb, 16), ("convw", convw, 64),
                          ("ldl", ldl, 16)):
        cvec[nm] = p.sb("c_" + nm, [128, wdt_])
        t.load(cvec[nm][:, :], ap_, writes=(nm,))
    AB = p.sb("c_AB", [128, 256])
    t.load(AB[:, :], A_l, writes=("ABa",))
    BB = p.sb("c_BB", [128, 256])
    t.load(BB[:, :], Bm_l, writes=("ABb",))
    hm = p.sb("c_hm", [128, 1])
    t.load(hm[:, :], hmask, writes=("hmask",))
    dl = p.sb("c_dl", [128, 16 * 4])
    delta, a_, C_ = dl[:, 0:16], dl[:, 16:32], dl[:, 32:48]
    t.op("vector", lambda e: e.tensor_scalar(
        out=dl[:, 48:64], in0=cvec["ldl"][:, :], scalar1=0.0, scalar2=None,
        op0=ALU.max), reads=("ldl",), writes=("dl",))
    t.op("scalar", lambda e: e.activation(
        out=delta, in_=cvec["ldl"][:, :], func=AF.Abs),
        reads=("ldl",), writes=("dl",))
    t.op("scalar", lambda e: e.activation(
        out=a_, in_=delta, func=AF.Exp, scale=-1.0), reads=("dl",), writes=("dl",))
    t.op("vector", lambda e: e.tensor_scalar(
        out=a_, in0=a_, scalar1=1.0, scalar2=None, op0=ALU.add),
        reads=("dl",), writes=("dl",))
    t.op("scalar", lambda e: e.activation(out=C_, in_=a_, func=AF.Ln),
         reads=("dl",), writes=("dl",))
    t.op("vector", lambda e: e.tensor_tensor(
        out=delta, in0=dl[:, 48:64], in1=C_, op=ALU.add),
        reads=("dl",), writes=("dl",))
    t.op("scalar", lambda e: e.activation(
        out=a_, in_=delta, func=AF.Exp, scale=-1.0), reads=("dl",), writes=("dl",))
    t.op("vector", lambda e: e.tensor_tensor(
        out=AB[:, :], in0=AB[:, :], in1=BB[:, :], op=ALU.mult),
        reads=("ABa", "ABb"), writes=("ABa",))
    for tt in range(16):
        t.op("vector", lambda e, tt=tt: e.tensor_reduce(
            out=C_[:, tt:tt + 1], in_=AB[:, tt * 16:(tt + 1) * 16], axis=AXX,
            op=ALU.add), reads=("ABa",), writes=("dl",))

    Z2B = [p.sb(f"z2b{k}", [128, SH], BF16) for k in range(8)]
    ztmp = p.sb("ztmp", [128, SH])
    for k in range(8):
        t.load(ztmp[:, :], z2f[k * 128:(k + 1) * 128, :], writes=("ztmp",))
        t.op("vector", lambda e, k=k: e.tensor_copy(
            out=Z2B[k][:, :], in_=ztmp[:, :]), reads=("ztmp",), writes=(f"z2b{k}",))

    ph = [p.ps(f"ph{i}", [128, 512], F32) for i in range(4)]
    XPs = [p.sb(f"xps{i}", [128, SH], BF16) for i in range(2)]
    XC = [p.sb(f"xc{m}", [128, SC], BF16) for m in range(16)]
    wxf = p.sb("wxf", [128, 8 * 128])
    wxb = p.sb("wxb", [128, 8 * 128], BF16)
    cv = [p.sb(f"cv{i}", [128, SC], BF16) for i in range(2)]
    chunks = _nchunks(SH)
    for m in range(16):
        t.load(wxf[:, :], bass.AP(wxt.tensor, m * D * 128,
                                  [[128, 128], [128 * 128, 8], [1, 128]]),
               writes=("wxf",))
        t.op("vector", lambda e: e.tensor_copy(out=wxb[:, :], in_=wxf[:, :]),
             reads=("wxf",), writes=("wxb",))
        xk = f"xps{m % 2}"
        for ci, (n0, wd) in enumerate(chunks):
            b = ci % 2
            pk = f"ph{b}"
            for k in range(8):
                t.op("tensor", lambda e, k=k, b=b: e.matmul(
                    ph[b][:, 0:wd], wxb[:, k * 128:(k + 1) * 128],
                    Z2B[k][:, n0:n0 + wd], start=(k == 0), stop=(k == 7)),
                    reads=("wxb", f"z2b{k}"), writes=(pk,))
            t.op("vector", lambda e, b=b, m=m, n0=n0, wd=wd: e.tensor_scalar(
                out=XPs[m % 2][:, n0:n0 + wd], in0=ph[b][:, 0:wd],
                scalar1=cvec["bxp"][:, m:m + 1], scalar2=None, op0=ALU.add),
                reads=(pk, "bxp"), writes=(pk, xk))
        t.op("vector", lambda e: e.tensor_scalar(
            out=XPs[m % 2][:, 125:128], in0=XPs[m % 2][:, 125:128],
            scalar1=hm[:, 0:1], scalar2=None, op0=ALU.mult),
            reads=(xk, "hmask"), writes=(xk,))
        t.op("vector", lambda e: e.tensor_scalar(
            out=cv[0][:, :], in0=XPs[m % 2][:, 125:125 + SC],
            scalar1=cvec["convw"][:, 4 * m:4 * m + 1], scalar2=None,
            op0=ALU.mult), reads=(xk, "convw"), writes=("cv0",))
        for kk in range(1, 4):
            src, dst = (kk + 1) % 2, kk % 2
            t.op("vector", lambda e, kk=kk, src=src, dst=dst: e.scalar_tensor_tensor(
                out=cv[dst][:, :], in0=XPs[m % 2][:, 125 + kk:125 + kk + SC],
                scalar=cvec["convw"][:, 4 * m + kk:4 * m + kk + 1],
                in1=cv[src][:, :], op0=ALU.mult, op1=ALU.add),
                reads=(xk, "convw", f"cv{src}"), writes=(f"cv{dst}",))
        t.op("scalar", lambda e, m=m: e.activation(
            out=XC[m][:, :], in_=cv[1][:, :], func=AF.Silu,
            bias=cvec["convb"][:, m:m + 1], scale=1.0),
            reads=("cv1", "convb"), writes=(f"xc{m}",))

    wcf = p.sb("wcf", [128, 16 * 128])
    wcb = p.sb("wcb", [128, 16 * 128], BF16)
    wdf = p.sb("wdf", [128, 16 * 128])
    wdb = p.sb("wdb", [128, 16 * 128], BF16)
    ct = p.sb("ct", [128, 512])
    dt_ = p.sb("dt", [128, 512])
    ub = p.sb("ub", [128, SC], BF16)
    gb = p.sb("gb", [128, SC])
    gc = p.sb("gc", [128, SC])
    ends = p.sb("ends", [128, 16])
    for m in range(16):
        t.load(wcf[:, :], bass.AP(wct.tensor, m * E * 128,
                                  [[128, 128], [128 * 128, 16], [1, 128]]),
               writes=("wcf",))
        t.op("vector", lambda e: e.tensor_copy(out=wcb[:, :], in_=wcf[:, :]),
             reads=("wcf",), writes=("wcb",))
        t.load(wdf[:, :], bass.AP(wdt.tensor, m * E * 128,
                                  [[128, 128], [128 * 128, 16], [1, 128]]),
               writes=("wdf",))
        t.op("vector", lambda e: e.tensor_copy(out=wdb[:, :], in_=wdf[:, :]),
             reads=("wdf",), writes=("wdb",))
        for (n0, wd) in _nchunks(SC):
            for k in range(16):
                t.op("tensor", lambda e, k=k: e.matmul(
                    ph[0][:, 0:wd], wcb[:, k * 128:(k + 1) * 128],
                    XC[k][:, n0:n0 + wd], start=(k == 0), stop=(k == 15)),
                    reads=("wcb", f"xc{k}"), writes=("ph0",))
            for k in range(16):
                t.op("tensor", lambda e, k=k: e.matmul(
                    ph[1][:, 0:wd], wdb[:, k * 128:(k + 1) * 128],
                    XC[k][:, n0:n0 + wd], start=(k == 0), stop=(k == 15)),
                    reads=("wdb", f"xc{k}"), writes=("ph1",))
            t.op("vector", lambda e, m=m, wd=wd: e.tensor_scalar(
                out=ct[:, 0:wd], in0=ph[0][:, 0:wd],
                scalar1=cvec["b_c"][:, m:m + 1], scalar2=None, op0=ALU.add),
                reads=("ph0", "b_c"), writes=("ph0", "ct"))
            t.op("scalar", lambda e, m=m: e.activation(
                out=dt_[:, 0:wd], in_=ph[1][:, 0:wd], func=AF.Sigmoid,
                bias=cvec["b_d"][:, m:m + 1], scale=1.0),
                reads=("ph1", "b_d"), writes=("ph1", "dt"))
            t.op("vector", lambda e, n0=n0, wd=wd: e.tensor_tensor(
                out=ub[:, n0:n0 + wd], in0=ct[:, 0:wd], in1=dt_[:, 0:wd],
                op=ALU.mult), reads=("ct", "dt"), writes=("ub",))
        t.op("vector", lambda e, m=m: e.tensor_tensor_scan(
            out=gb[:, :], data0=bcast_col(a_[:, m:m + 1], SC), data1=ub[:, :],
            initial=0.0, op0=ALU.mult, op1=ALU.add),
            reads=("ub", "dl"), writes=("gb",))
        t.op("vector", lambda e, m=m: e.tensor_copy(
            out=ends[:, m:m + 1], in_=gb[:, SC - 1:SC]),
            reads=("gb",), writes=("ends",))
        t.op("vector", lambda e, m=m: e.tensor_scalar(
            out=gc[:, :], in0=gb[:, :], scalar1=C_[:, m:m + 1], scalar2=None,
            op0=ALU.mult), reads=("gb", "dl"), writes=("gc",))
        t.store(o_g[m * 128:(m + 1) * 128, :], gc[:, :], reads=("gc",))
    t.store(o_ends, ends[:, :], reads=("ends",))
    return p.close()


def build_l5():
    p = Prog("l5")
    i_g = p.inp("i_g", [E, SC])
    z2f = p.inp("z2f", [D, SH])
    i_z1 = p.inp("i_z1", [SC, D])
    i_xp = p.inp("i_xp", [SH, D])
    ends_all = p.inp("ends_all", [NCORES * 128, 16])
    kp = p.inp("kp", [1, 16])
    wgt = p.inp("wgt", [16, D, 128])
    wot = p.inp("wot", [E, D])
    bg = p.inp("bg", [128, 16])
    bomr = p.inp("bomr", [1, D])
    ldl = p.inp("ldl", [128, 16])
    A_l = p.inp("A_l", [128, 256])
    Bm_l = p.inp("Bm_l", [128, 256])
    o_xpp = p.out("o_xpp", [SC, D])
    o_a3 = p.out("o_a3", [1, 1])
    t = p.t

    ones, w_col = emit_witness(p)
    cbg = p.sb("c_bg", [128, 16])
    t.load(cbg[:, :], bg, writes=("bg",))
    cld = p.sb("c_ldl", [128, 16])
    t.load(cld[:, :], ldl, writes=("ldl",))
    AB = p.sb("c_AB", [128, 256])
    t.load(AB[:, :], A_l, writes=("ABa",))
    BB = p.sb("c_BB", [128, 256])
    t.load(BB[:, :], Bm_l, writes=("ABb",))
    bo = p.sb("borow", [1, D])
    t.load(bo[:, :], bomr, writes=("borow",))
    bob = p.sb("borowb", [1, D], BF16)
    t.op("vector", lambda e: e.tensor_copy(out=bob[:, :], in_=bo[:, :]),
         reads=("borow",), writes=("borowb",))
    onesb = p.sb("onesb", [1, 128], BF16)
    t.op("vector", lambda e: e.tensor_copy(out=onesb[:, :], in_=ones[:, :]),
         reads=("ones",), writes=("onesb",))
    dl = p.sb("c_dl", [128, 16 * 4])
    delta, a_, C_ = dl[:, 0:16], dl[:, 16:32], dl[:, 32:48]
    t.op("vector", lambda e: e.tensor_scalar(
        out=dl[:, 48:64], in0=cld[:, :], scalar1=0.0, scalar2=None,
        op0=ALU.max), reads=("ldl",), writes=("dl",))
    t.op("scalar", lambda e: e.activation(out=delta, in_=cld[:, :], func=AF.Abs),
         reads=("ldl",), writes=("dl",))
    t.op("scalar", lambda e: e.activation(
        out=a_, in_=delta, func=AF.Exp, scale=-1.0), reads=("dl",), writes=("dl",))
    t.op("vector", lambda e: e.tensor_scalar(
        out=a_, in0=a_, scalar1=1.0, scalar2=None, op0=ALU.add),
        reads=("dl",), writes=("dl",))
    t.op("scalar", lambda e: e.activation(out=C_, in_=a_, func=AF.Ln),
         reads=("dl",), writes=("dl",))
    t.op("vector", lambda e: e.tensor_tensor(
        out=delta, in0=dl[:, 48:64], in1=C_, op=ALU.add),
        reads=("dl",), writes=("dl",))
    t.op("scalar", lambda e: e.activation(
        out=a_, in_=delta, func=AF.Exp, scale=-1.0), reads=("dl",), writes=("dl",))
    t.op("vector", lambda e: e.tensor_tensor(
        out=AB[:, :], in0=AB[:, :], in1=BB[:, :], op=ALU.mult),
        reads=("ABa", "ABb"), writes=("ABa",))
    for tt in range(16):
        t.op("vector", lambda e, tt=tt: e.tensor_reduce(
            out=C_[:, tt:tt + 1], in_=AB[:, tt * 16:(tt + 1) * 16], axis=AXX,
            op=ALU.add), reads=("ABa",), writes=("dl",))

    kr = p.sb("kp_r", [1, 16])
    t.load(kr[:, :], kp, writes=("kp",))
    KB = p.sb("kb", [128, 16])
    psc = p.ps("pscal", [128, 16], F32)
    t.op("tensor", lambda e: e.matmul(
        psc[:, 0:16], ones[:, :], kr[0:1, 0:16], start=True, stop=True),
        reads=("ones", "kp"), writes=("pscal",))
    t.op("scalar", lambda e: e.activation(out=KB[:, :], in_=psc[:, 0:16],
                                          func=AF.Copy),
         reads=("pscal",), writes=("kb",))
    ej = p.sb("ej", [128, 16])
    cf = p.sb("cf", [128, 16])
    carry = p.sb("carry", [128, 16 * 2])
    t.op("vector", lambda e: e.memset(carry[:, 0:16], 0.0), writes=("carry",))
    for j in range(NCORES):
        t.load(ej[:, :], ends_all[j * 128:(j + 1) * 128, :], writes=("ej",))
        t.op("scalar", lambda e, j=j: e.activation(
            out=cf[:, :], in_=delta, func=AF.Exp, scale=KB[:, j:j + 1]),
            reads=("dl", "kb"), writes=("cf",))
        t.op("vector", lambda e, j=j: e.tensor_scalar(
            out=cf[:, :], in0=cf[:, :], scalar1=KB[:, 8 + j:9 + j], scalar2=None,
            op0=ALU.mult), reads=("cf", "kb"), writes=("cf",))
        t.op("vector", lambda e: e.tensor_tensor(
            out=cf[:, :], in0=cf[:, :], in1=ej[:, :], op=ALU.mult),
            reads=("cf", "ej"), writes=("cf",))
        t.op("vector", lambda e: e.tensor_tensor(
            out=carry[:, 0:16], in0=carry[:, 0:16], in1=cf[:, :], op=ALU.add),
            reads=("carry", "cf"), writes=("carry",))

    Z2B = [p.sb(f"z2b{k}", [128, SC], BF16) for k in range(8)]
    ztmp = p.sb("ztmp", [128, SC])
    for k in range(8):
        t.load(ztmp[:, :], z2f[k * 128:(k + 1) * 128, HALO:SH], writes=("ztmp",))
        t.op("vector", lambda e, k=k: e.tensor_copy(
            out=Z2B[k][:, :], in_=ztmp[:, :]), reads=("ztmp",), writes=(f"z2b{k}",))

    ph = [p.ps(f"ph{i}", [128, 512], F32) for i in range(2)]
    XG = [p.sb(f"xg{m}", [128, SC], BF16) for m in range(16)]
    zc = p.sb("zc", [128, 1])
    t.op("vector", lambda e: e.memset(zc[:, :], 0.0), writes=("zc",))
    crb = p.sb("crb", [128, SC])
    gbr = p.sb("gbr", [128, SC])
    t1b = p.sb("t1b", [128, SC])
    gs = p.sb("gs", [128, SC], BF16)
    wgf = p.sb("wgf", [128, 8 * 128])
    wgb = p.sb("wgb", [128, 8 * 128], BF16)
    for m in range(16):
        t.op("vector", lambda e, m=m: e.tensor_tensor_scan(
            out=crb[:, :], data0=bcast_col(a_[:, m:m + 1], SC),
            data1=bcast_col(zc[:, 0:1], SC),
            initial=carry[:, m:m + 1], op0=ALU.mult, op1=ALU.add),
            reads=("dl", "zc", "carry"), writes=("crb",))
        t.load(gbr[:, :], i_g[m * 128:(m + 1) * 128, :], writes=("gbr",))
        t.op("vector", lambda e, m=m: e.scalar_tensor_tensor(
            out=t1b[:, :], in0=crb[:, :], scalar=C_[:, m:m + 1], in1=gbr[:, :],
            op0=ALU.mult, op1=ALU.add), reads=("crb", "dl", "gbr"), writes=("t1b",))
        t.load(wgf[:, :], bass.AP(wgt.tensor, m * D * 128,
                                  [[128, 128], [128 * 128, 8], [1, 128]]),
               writes=("wgf",))
        t.op("vector", lambda e: e.tensor_copy(out=wgb[:, :], in_=wgf[:, :]),
             reads=("wgf",), writes=("wgb",))
        for ci, (n0, wd) in enumerate(_nchunks(SC)):
            b = ci % 2
            pk = f"ph{b}"
            for k in range(8):
                t.op("tensor", lambda e, k=k, b=b: e.matmul(
                    ph[b][:, 0:wd], wgb[:, k * 128:(k + 1) * 128],
                    Z2B[k][:, n0:n0 + wd], start=(k == 0), stop=(k == 7)),
                    reads=("wgb", f"z2b{k}"), writes=(pk,))
            t.op("scalar", lambda e, m=m, b=b, n0=n0, wd=wd: e.activation(
                out=gs[:, n0:n0 + wd], in_=ph[b][:, 0:wd], func=AF.Sigmoid,
                bias=cbg[:, m:m + 1], scale=1.0),
                reads=(pk, "bg"), writes=(pk, "gs"))
        t.op("vector", lambda e, m=m: e.tensor_tensor(
            out=XG[m][:, :], in0=t1b[:, :], in1=gs[:, :], op=ALU.mult),
            reads=("t1b", "gs"), writes=(f"xg{m}",))

    WOB = [p.sb(f"wob{k}", [128, D], BF16) for k in range(16)]
    wtmp = p.sb("wtmp", [128, D])
    for k in range(16):
        t.load(wtmp[:, :], wot[k * 128:(k + 1) * 128, :], writes=("wtmp",))
        t.op("vector", lambda e, k=k: e.tensor_copy(
            out=WOB[k][:, :], in_=wtmp[:, :]), reads=("wtmp",), writes=(f"wob{k}",))
    ow = p.sb("ow", [128, D])
    xt2 = p.sb("xt2", [128, D])
    z1t = p.sb("z1t", [128, D])
    yt = p.sb("yt", [128, D])
    lntmp = p.sb("clntmp", [128, D])
    lstat = p.sb("clstat", [128, 8])
    mx = p.sb("mx3", [128, 2])
    t.op("vector", lambda e: e.memset(mx[:, 0:1], 0.0), writes=("mx3",))
    for j in range(NT_R):
        for nn in range(2):
            pk = f"ph{nn}"
            for k in range(16):
                t.op("tensor", lambda e, k=k, nn=nn, j=j: e.matmul(
                    ph[nn][:, 0:512], XG[k][:, j * 128:(j + 1) * 128],
                    WOB[k][:, nn * 512:(nn + 1) * 512],
                    start=(k == 0), stop=False),
                    reads=(f"xg{k}", f"wob{k}"), writes=(pk,))
            t.op("tensor", lambda e, nn=nn: e.matmul(
                ph[nn][:, 0:512], onesb[0:1, :], bob[0:1, nn * 512:(nn + 1) * 512],
                start=False, stop=True), reads=("onesb", "borowb"), writes=(pk,))
            t.op("scalar", lambda e, nn=nn: e.activation(
                out=ow[:, nn * 512:(nn + 1) * 512], in_=ph[nn][:, 0:512],
                func=AF.Copy), reads=(pk,), writes=(pk, "ow"))
        t.load(xt2[:, :], i_xp[HALO + j * 128:HALO + (j + 1) * 128, :],
               writes=("xt2",))
        t.load(z1t[:, :], i_z1[j * 128:(j + 1) * 128, :], writes=("z1t",))
        t.op("vector", lambda e: e.tensor_tensor(
            out=ow[:, :], in0=ow[:, :], in1=xt2[:, :], op=ALU.add),
            reads=("ow", "xt2"), writes=("ow",))
        t.op("vector", lambda e: e.tensor_tensor(
            out=ow[:, :], in0=ow[:, :], in1=z1t[:, :], op=ALU.add),
            reads=("ow", "z1t"), writes=("ow",))
        t.store(o_xpp[j * 128:(j + 1) * 128, :], ow[:, :], reads=("ow",))
        m1, r1 = emit_ln_tile(p, ow[:, :], "ow", lntmp[:, :], lstat)
        t.op("vector", lambda e: e.tensor_scalar(
            out=yt[:, :], in0=ow[:, :], scalar1=m1, scalar2=r1,
            op0=ALU.subtract, op1=ALU.mult), reads=("ow", "lnstat"), writes=("yt",))
        t.op("vector", lambda e: e.tensor_reduce(
            out=mx[:, 1:2], in_=yt[:, :], axis=AXX, op=ALU.max,
            apply_absolute_value=True), reads=("yt",), writes=("mx3",))
        t.op("vector", lambda e: e.tensor_tensor(
            out=mx[:, 0:1], in0=mx[:, 0:1], in1=mx[:, 1:2], op=ALU.max),
            reads=("mx3",), writes=("mx3",))
    emit_amax_scalar(p, mx[:, 0:1], "mx3", o_a3, "scr3")
    return p.close()



def _prep_weights(iv):
    f32 = np.float32
    w = {}
    w["w1t"] = np.ascontiguousarray(np.asarray(iv["L0_ffn_W_in"], f32).T)
    w["w1o"] = np.ascontiguousarray(np.asarray(iv["L0_ffn_W_out"], f32))
    win = np.asarray(iv["L0_attn_W_in"], f32)
    w["wxt_t"] = np.ascontiguousarray(
        win[:E].T.reshape(D, 16, 128).transpose(1, 0, 2))
    w["wgt_t"] = np.ascontiguousarray(
        win[E:].T.reshape(D, 16, 128).transpose(1, 0, 2))
    w["wct_t"] = np.ascontiguousarray(
        np.asarray(iv["L0_attn_W_c"], f32).T.reshape(E, 16, 128).transpose(1, 0, 2))
    w["wdt_t"] = np.ascontiguousarray(
        np.asarray(iv["L0_attn_W_d"], f32).T.reshape(E, 16, 128).transpose(1, 0, 2))
    w["wot"] = np.ascontiguousarray(np.asarray(iv["L0_attn_W_out"], f32).T)
    w["b_out_row"] = np.ascontiguousarray(
        np.asarray(iv["L0_ffn_b_out"], f32).reshape(1, D))
    w["bom_row"] = np.ascontiguousarray(
        np.asarray(iv["L0_attn_b_out"], f32).reshape(1, D))

    def vec(v, n):
        return np.ascontiguousarray(np.asarray(v, f32).reshape(n, 128).T)

    w["b_in_f"] = vec(iv["L0_ffn_b_in"], 32)
    w["b_out_f"] = vec(iv["L0_ffn_b_out"], 8)
    w["ln_g"] = vec(iv["L0_attn_ln_g"], 8)
    w["ln_b"] = vec(iv["L0_attn_ln_b"], 8)
    bi = np.asarray(iv["L0_attn_b_in"], f32)
    w["bxp"] = vec(bi[:E], 16)
    w["bg"] = vec(bi[E:], 16)
    w["b_c"] = vec(iv["L0_attn_b_c"], 16)
    w["b_d"] = vec(iv["L0_attn_b_d"], 16)
    w["convb"] = vec(iv["L0_attn_conv_b"], 16)
    w["ld"] = vec(iv["L0_attn_log_delta"], 16)
    w["bom"] = vec(iv["L0_attn_b_out"], 8)
    cw = np.asarray(iv["L0_attn_conv_w"], f32)[:, 0, :]
    w["convw"] = np.ascontiguousarray(
        cw.reshape(16, 128, 4).transpose(1, 0, 2).reshape(128, 64))
    w["A_l"] = np.ascontiguousarray(
        np.asarray(iv["L0_attn_A"], f32).reshape(16, 128, 16)
        .transpose(1, 0, 2).reshape(128, 256))
    w["Bm_l"] = np.ascontiguousarray(
        np.asarray(iv["L0_attn_Bm"], f32).reshape(16, 128, 16)
        .transpose(1, 0, 2).reshape(128, 256))
    return w


def _shard_x(x):
    x2 = np.ascontiguousarray(np.asarray(x, np.float32).reshape(S, D))
    xs = []
    for c in range(NCORES):
        lo = c * SC - HALO
        if lo < 0:
            sl = np.concatenate(
                [np.zeros((HALO, D), np.float32), x2[0:(c + 1) * SC]], axis=0)
        else:
            sl = x2[lo:(c + 1) * SC]
        xs.append(np.ascontiguousarray(sl))
    return xs


def _gather_scalar(res, key):
    return np.ascontiguousarray(
        np.concatenate([np.asarray(res.results[c][key]).reshape(1, 1)
                        for c in range(NCORES)], axis=1))


_NC_CACHE = {}


def _run(tag, build_fn, in_maps):
    if tag not in _NC_CACHE:
        _NC_CACHE[tag] = build_fn()
    return run_bass_kernel_spmd(_NC_CACHE[tag], in_maps, core_ids=list(range(NCORES)))


def _kp_rows():
    rows = []
    for c in range(NCORES):
        kpow = [(-2048.0 * (c - 1 - j)) if j < c else 0.0 for j in range(NCORES)]
        mask = [1.0 if j < c else 0.0 for j in range(NCORES)]
        rows.append(np.ascontiguousarray(
            np.array([kpow + mask], dtype=np.float32)))
    return rows


def kernel(**inputs):
    w = _prep_weights(inputs)
    xsh = _shard_x(inputs["L0_attn_x"])

    r1 = _run("l1", build_l1, [dict(xs=xsh[c]) for c in range(NCORES)])
    am1 = _gather_scalar(r1, "o_a1")

    r2 = _run("l2", lambda: build_l2(NT_H, 1, "l2"),
              [dict(xs=xsh[c], am1=am1, w1t=w["w1t"], b_in_f=w["b_in_f"])
               for c in range(NCORES)])
    am2 = _gather_scalar(r2, "o_a2")

    r3 = _run("l3", lambda: build_ffn_out("l3", NT_H, True),
              [dict(xs=xsh[c], i_h=np.ascontiguousarray(r2.results[c]["o_h"]),
                    am2=am2, w1o=w["w1o"], b_out_row=w["b_out_row"],
                    ln_g=w["ln_g"], ln_b=w["ln_b"]) for c in range(NCORES)])

    r4 = _run("l4", build_l4,
              [dict(z2f=np.ascontiguousarray(r3.results[c]["o_z2"]),
                    wxt=w["wxt_t"], wct=w["wct_t"], wdt=w["wdt_t"],
                    bxp=w["bxp"], b_c=w["b_c"], b_d=w["b_d"],
                    convb=w["convb"], convw=w["convw"], ldl=w["ld"],
                    A_l=w["A_l"], Bm_l=w["Bm_l"],
                    hmask=np.full((128, 1), 0.0 if c == 0 else 1.0, np.float32))
               for c in range(NCORES)])
    ends_all = np.ascontiguousarray(np.concatenate(
        [np.asarray(r4.results[c]["o_ends"]) for c in range(NCORES)], axis=0))
    kps = _kp_rows()

    r5 = _run("l5", build_l5,
              [dict(i_g=np.ascontiguousarray(r4.results[c]["o_g"]),
                    z2f=np.ascontiguousarray(r3.results[c]["o_z2"]),
                    i_z1=np.ascontiguousarray(r3.results[c]["o_z1"]),
                    i_xp=np.ascontiguousarray(r3.results[c]["o_xp"]),
                    ends_all=ends_all, kp=kps[c], wgt=w["wgt_t"], wot=w["wot"],
                    bg=w["bg"], bomr=w["bom_row"], ldl=w["ld"],
                    A_l=w["A_l"], Bm_l=w["Bm_l"]) for c in range(NCORES)])
    am3 = _gather_scalar(r5, "o_a3")
    xpp = [np.ascontiguousarray(r5.results[c]["o_xpp"]) for c in range(NCORES)]

    r6 = _run("l6", lambda: build_l2(NT_R, 0, "l6"),
              [dict(xs=xpp[c], am1=am3, w1t=w["w1t"], b_in_f=w["b_in_f"])
               for c in range(NCORES)])
    am4 = _gather_scalar(r6, "o_a2")

    r7 = _run("l7", lambda: build_ffn_out("l7", NT_R, False),
              [dict(xs=xpp[c], i_h=np.ascontiguousarray(r6.results[c]["o_h"]),
                    am2=am4, w1o=w["w1o"], b_out_row=w["b_out_row"])
               for c in range(NCORES)])
    out = np.concatenate([np.asarray(r7.results[c]["o_out"])
                          for c in range(NCORES)], axis=0)
    return out.reshape(1, S, D).astype(np.float32, copy=False)
